# version 01
# baseline (speedup 1.0000x reference)
import math
import os
from contextlib import ExitStack
import numpy as np
import concourse.bass as bass
import concourse.mybir as mybir
from concourse.bass_utils import run_bass_kernel_spmd

F32 = mybir.dt.float32
BF16 = mybir.dt.bfloat16
I32 = mybir.dt.int32
AF = mybir.ActivationFunctionType
ALU = mybir.AluOpType
AX = mybir.AxisListType
ds = bass.ds

NCORES = 8
B = 8
N = 8192
D = 1024
TOK = N // NCORES
HEADS, DH, M = 8, 64, 256
PINV_ITERS = 6
CONV_K = 33
QMAX = 127.0
RND_C = 1.5 * 2.0 ** 23
EPS = 1e-5
IFI = 2816

WG_SH = IFI // NCORES
WD_SH = D // NCORES


def _bass():
    return bass.Bass("TRN2", target_bir_lowering=False)


class Chain:

    def __init__(self, eng, semh):
        self.eng, self.sem, self.n = eng, semh, 0

    def __call__(self, inst):
        self.n += 1
        inst.then_inc(self.sem, 1)
        self.eng.wait_ge(self.sem, self.n)
        return inst



def build_rA():
    nc = _bass()
    xsh = nc.dram_tensor("xsh", [B, TOK, D], F32, kind="ExternalInput")
    g32 = nc.dram_tensor("g32", [128, 4], F32, kind="ExternalInput")
    wg_sh = nc.dram_tensor("wg_sh", [WG_SH, D], F32, kind="ExternalInput")
    wu_sh = nc.dram_tensor("wu_sh", [WG_SH, D], F32, kind="ExternalInput")
    wd_sh = nc.dram_tensor("wd_sh", [WD_SH, IFI], F32, kind="ExternalInput")

    ln1sp = nc.dram_tensor("ln1sp", [B, TOK, D], BF16, kind="ExternalOutput")
    xlp = nc.dram_tensor("xlp", [B, 32, D], F32, kind="ExternalOutput")
    wgq = nc.dram_tensor("wgq", [WG_SH, D], BF16, kind="ExternalOutput")
    wuq = nc.dram_tensor("wuq", [WG_SH, D], BF16, kind="ExternalOutput")
    wdq = nc.dram_tensor("wdq", [WD_SH, IFI], BF16, kind="ExternalOutput")
    wgs = nc.dram_tensor("wgs", [WG_SH, 1], F32, kind="ExternalOutput")
    wus = nc.dram_tensor("wus", [WG_SH, 1], F32, kind="ExternalOutput")
    wds = nc.dram_tensor("wds", [WD_SH, 1], F32, kind="ExternalOutput")

    NT = B * (TOK // 128)
    wjobs = []
    for (win, wqo, wso, rows, cols) in (
        (wg_sh, wgq, wgs, WG_SH, D),
        (wu_sh, wuq, wus, WG_SH, D),
        (wd_sh, wdq, wds, WD_SH, IFI),
    ):
        r0 = 0
        while r0 < rows:
            r = min(128, rows - r0)
            wjobs.append((win, wqo, wso, r0, r, cols))
            r0 += r
    NW = len(wjobs)

    with ExitStack() as ctx:
        def sb(name, shape, dt=F32):
            return ctx.enter_context(nc.sbuf_tensor(name, shape, dt))

        def psum(name, shape, dt=F32):
            return ctx.enter_context(nc.psum_tensor(name, shape, dt))

        def sem(name):
            return ctx.enter_context(nc.semaphore(name))

        xb = sb("xb", [128, 2 * D])
        lnb = sb("lnb", [128, 2 * D], BF16)
        g32f = sb("g32f", [128, 4])
        g32b = sb("g32b", [128, 4], BF16)
        stat = sb("stat", [128, 16])
        stdv = sb("stdv", [128, 2])
        rstdv = sb("rstdv", [128, 2])
        sqscr = sb("sqscr", [128, D])
        epsv = sb("epsv", [128, 2])
        stg = sb("stg", [4, 2 * D])
        wbuf = sb("wbuf", [128, 2 * IFI])
        wq_o = sb("wq_o", [128, 2 * IFI], BF16)
        wsc = sb("wsc", [128, 8])
        pxl0 = psum("pxl0", [4, 512])
        pxl1 = psum("pxl1", [4, 512])
        ldp = [sem("ld0"), sem("ld1")]
        s_gld = sem("s_gld")
        wlp = [sem("wl0"), sem("wl1")]
        stl = [sem("stl0"), sem("stl1")]
        stx = [sem("stx0"), sem("stx1")]
        wsp = [sem("ws0"), sem("ws1")]
        s_sq = sem("s_sq")
        s_var = sem("s_var")
        s_std = sem("s_std")
        s_stat2 = sem("s_stat2")
        s_norm = sem("s_norm")
        s_pe = sem("s_pe")
        s_xl = sem("s_xl")
        w_q = sem("w_q")
        s_ch = sem("s_ch")
        s_g32 = sem("s_g32")
        blk = ctx.enter_context(nc.Block())

        @blk.sync
        def _(sync):
            sync.dma_start(out=g32f[:, :], in_=g32[:, :]).then_inc(s_gld, 16)
            for i in range(NT):
                p = i % 2
                b, t = divmod(i, TOK // 128)
                if i >= 2:
                    sync.wait_ge(s_norm, i - 1)
                sync.dma_start(out=xb[:, p * D:(p + 1) * D],
                               in_=xsh[b, t * 128:(t + 1) * 128, :]
                               ).then_inc(ldp[p], 16)
            for j, (win, wqo, wso, r0, r, cols) in enumerate(wjobs):
                p = j % 2
                if j >= 2:
                    sync.wait_ge(w_q, 2 * (j - 1))
                sync.dma_start(out=wbuf[0:r, p * IFI: p * IFI + cols],
                               in_=win[r0:r0 + r, :]).then_inc(wlp[p], 16)

        @blk.vector
        def _(v):
            ch = Chain(v, s_ch)
            v.memset(epsv[:, :], EPS).then_inc(s_std, 1000)
            v.wait_ge(s_gld, 16)
            v.tensor_copy(g32b[:, :], g32f[:, :]).then_inc(s_g32, 1)
            v.wait_ge(s_g32, 1)
            for i in range(NT):
                p = i % 2
                c = p * 8
                v.wait_ge(ldp[p], 16 * (i // 2 + 1))
                if i >= 2:
                    v.wait_ge(s_norm, i - 1)
                xv = xb[:, p * D:(p + 1) * D]
                s1 = stat[:, c + 1: c + 2]
                mu = stat[:, c + 2: c + 3]
                var = stat[:, c + 3: c + 4]
                std = stdv[:, p: p + 1]
                tmp = stat[:, c + 5: c + 6]
                rstd = rstdv[:, p: p + 1]
                nmr = stat[:, c + 7: c + 8]
                msq = stat[:, c + 4: c + 5]
                ch(v.tensor_reduce(s1, xv, AX.X, ALU.add))
                v.wait_ge(s_sq, i + 1)
                ch(v.tensor_scalar(mu, s1, 1.0 / D, None, ALU.mult))
                ch(v.tensor_scalar(msq, stat[:, c + 0: c + 1], 1.0 / D, None, ALU.mult))
                ch(v.tensor_tensor(tmp, mu, mu, ALU.mult))
                v.tensor_tensor(var, msq, tmp, ALU.subtract).then_inc(s_var, 1)
                v.wait_ge(s_std, 1000 + i + 1)
                ch(v.reciprocal(rstd, std))
                v.tensor_scalar(nmr, mu, rstd, -1.0,
                                ALU.mult, ALU.mult).then_inc(s_stat2, 1)
                v.wait_ge(s_pe, i + 1)
                if i >= 2:
                    v.wait_ge(stx[p], 16 * (i // 2))
                v.tensor_copy(stg[:, p * D: p * D + 512], pxl0[:, :]).then_inc(s_xl, 1)
                v.tensor_copy(stg[:, p * D + 512: p * D + 1024],
                              pxl1[:, :]).then_inc(s_xl, 1)
            for j, (win, wqo, wso, r0, r, cols) in enumerate(wjobs):
                p = j % 2
                cc = p * 4
                v.wait_ge(wlp[p], 16 * (j // 2 + 1))
                if j >= 2:
                    v.wait_ge(wsp[p], 32 * (j // 2))
                wv = wbuf[0:r, p * IFI: p * IFI + cols]
                amax = wsc[0:r, cc + 0: cc + 1]
                inv = wsc[0:r, cc + 2: cc + 3]
                ch(v.tensor_reduce(amax, wv, AX.X, ALU.max, apply_absolute_value=True))
                ch(v.reciprocal(inv, amax))
                ch(v.tensor_scalar(inv, inv, QMAX, None, ALU.mult))
                ch(v.tensor_scalar(wv, wv, inv, RND_C, ALU.mult, ALU.add))
                ch(v.tensor_scalar(wv, wv, RND_C, QMAX, ALU.subtract, ALU.min))
                v.tensor_scalar(wq_o[0:r, p * IFI: p * IFI + cols], wv,
                                -QMAX - 1.0, None, ALU.max).then_inc(w_q, 1)
                v.tensor_scalar(wsc[0:r, cc + 1: cc + 2], amax, 1.0 / QMAX, None,
                                ALU.mult).then_inc(w_q, 1)

        @blk.scalar
        def _(act):
            for i in range(NT):
                p = i % 2
                c = p * 8
                act.wait_ge(ldp[p], 16 * (i // 2 + 1))
                act.activation(sqscr[:, :], xb[:, p * D:(p + 1) * D], AF.Square,
                               accum_out=stat[:, c + 0: c + 1]).then_inc(s_sq, 1)
                act.wait_ge(s_var, i + 1)
                act.activation(stdv[:, p: p + 1], stat[:, c + 3: c + 4],
                               AF.Sqrt, bias=epsv[:, 0:1]).then_inc(s_std, 1)
                act.wait_ge(s_stat2, i + 1)
                if i >= 2:
                    act.wait_ge(stl[p], 16 * (i // 2))
                act.activation(lnb[:, p * D:(p + 1) * D], xb[:, p * D:(p + 1) * D],
                               AF.Identity, scale=rstdv[:, p: p + 1],
                               bias=stat[:, c + 7: c + 8]).then_inc(s_norm, 1)

        @blk.gpsimd
        def _(g):
            for i in range(NT):
                p = i % 2
                b, t = divmod(i, TOK // 128)
                g.wait_ge(s_norm, i + 1)
                g.dma_start(out=ln1sp[b, t * 128:(t + 1) * 128, :],
                            in_=lnb[:, p * D:(p + 1) * D]).then_inc(stl[p], 16)
                g.wait_ge(s_xl, 2 * (i + 1))
                g.dma_start(out=xlp[b, t * 4:(t + 1) * 4, :],
                            in_=stg[:, p * D:(p + 1) * D]).then_inc(stx[p], 16)
            for j, (win, wqo, wso, r0, r, cols) in enumerate(wjobs):
                p = j % 2
                cc = p * 4
                g.wait_ge(w_q, 2 * (j + 1))
                g.dma_start(out=wqo[r0:r0 + r, :],
                            in_=wq_o[0:r, p * IFI: p * IFI + cols]).then_inc(wsp[p], 16)
                g.dma_start(out=wso[r0:r0 + r, :],
                            in_=wsc[0:r, cc + 1: cc + 2]).then_inc(wsp[p], 16)
            g.wait_ge(stl[0], 16 * (NT // 2))
            g.wait_ge(stl[1], 16 * (NT // 2))
            g.wait_ge(stx[0], 16 * (NT // 2))
            g.wait_ge(stx[1], 16 * (NT // 2))
            g.wait_ge(wsp[0], 32 * ((NW + 1) // 2))
            g.wait_ge(wsp[1], 32 * (NW // 2))

        @blk.tensor
        def _(pe):
            pe.wait_ge(s_g32, 1)
            for i in range(NT):
                p = i % 2
                pe.wait_ge(s_norm, i + 1)
                if i >= 1:
                    pe.wait_ge(s_xl, 2 * i)
                pe.matmul(pxl0[:, :], g32b[:, :], lnb[:, p * D: p * D + 512],
                          start=True, stop=True)
                pe.matmul(pxl1[:, :], g32b[:, :], lnb[:, p * D + 512: p * D + 1024],
                          start=True, stop=True).then_inc(s_pe, 1)

    return nc


def host_rA(inputs):
    x = inputs["L0_attn_dense"]
    wg = inputs["L0_ffn_w_gate"]
    wu = inputs["L0_ffn_w_up"]
    wd = inputs["L0_ffn_w_down"]
    g32 = np.zeros((128, 4), np.float32)
    for t in range(128):
        g32[t, t // 32] = 1.0 / 32.0
    maps = []
    for c in range(NCORES):
        maps.append({
            "xsh": np.ascontiguousarray(x[:, c * TOK:(c + 1) * TOK, :]),
            "g32": g32,
            "wg_sh": np.ascontiguousarray(wg[c * WG_SH:(c + 1) * WG_SH, :]),
            "wu_sh": np.ascontiguousarray(wu[c * WG_SH:(c + 1) * WG_SH, :]),
            "wd_sh": np.ascontiguousarray(wd[c * WD_SH:(c + 1) * WD_SH, :]),
        })
    return maps




def build_rB(stages=None):
    if stages is None:
        stages = 7
    sub, nxl, xlo = 9, 16, 0
    nc = _bass()
    xlf = nc.dram_tensor("xlf", [2048, D], F32, kind="ExternalInput")
    ln0 = nc.dram_tensor("ln0", [N, D], BF16, kind="ExternalInput")
    qkvw = nc.dram_tensor("qkvw", [D, 1024], F32, kind="ExternalInput")
    qkvwh = nc.dram_tensor("qkvwh", [D, 192], F32, kind="ExternalInput")
    tope = nc.dram_tensor("tope", [3, 128, 128], F32, kind="ExternalInput")
    iden = nc.dram_tensor("iden", [128, 128], F32, kind="ExternalInput")
    i256 = nc.dram_tensor("i256", [256, 256], F32, kind="ExternalInput")

    outc = nc.dram_tensor("outc", [N, DH], F32, kind="ExternalOutput")
    bdbg = nc.dram_tensor("bdbg", [1, 8], F32, kind="ExternalOutput")

    NBLK = 4
    NTT = N // 128
    TPB = NTT // NBLK

    PE1A, PE1B, PE1C = 0, 128, 160
    PE2A, PE2B, PE2C, PE2D = 162, 290, 291, 292
    PE3A, PE3I = 293, 297
    PE4 = 369
    PE5 = 481
    PE6A = 609
    PE6B = 673
    PE6C = 802
    PE7 = 803
    DV1A, DV1B, DV1C = 0, 128, 160
    DV2A, DV2B, DV2C, DV2D = 162, 290, 291, 292
    DV3A, DV3B, DV3I = 293, 297, 299
    DV4 = 395
    DV5 = 507
    DV6A, DV6B, DV6C = 635, 699, 700
    DV6D, DV6E = 829, 830
    DV7 = 830
    AC2, AC2H, AC5, AC6 = 0, 128, 130, 194

    with ExitStack() as ctx:
        def sb(name, shape, dt=F32):
            return ctx.enter_context(nc.sbuf_tensor(name, shape, dt))

        def psum(name, shape, dt=F32):
            return ctx.enter_context(nc.psum_tensor(name, shape, dt))

        def sem(name):
            return ctx.enter_context(nc.semaphore(name))

        ar1 = sb("ar1", [128, 16384], BF16)
        ar2 = sb("ar2", [128, 16384], BF16)
        qkvw_b = sb("qkvw_b", [128, 8192], BF16)
        qkvwh_b = sb("qkvwh_b", [128, 1536], BF16)
        qlkl = sb("qlkl", [128, 16384], BF16)
        qkn = sb("qkn", [64, N], BF16)
        vnb = sb("vnb", [64, N], BF16)
        vnt = sb("vnt", [128, 64 * 64], BF16)
        fscr = sb("fscr", [128, 512], F32)
        bscr = sb("bscr", [128, 512], BF16)
        idb = sb("idb", [128, 128], BF16)
        idf = sb("idf", [128, 128], F32)
        topb = sb("topb", [128, 3 * 128], BF16)
        qlh = sb("qlh", [64, 512], BF16)
        qlhf = sb("qlhf", [64, 512], F32)
        amat = sb("amat", [128, 512], F32)
        atm = sb("atm", [128, 512], F32)
        zb = sb("zb", [128, 512], F32)
        ztb = sb("ztb", [128, 512], F32)
        azb = sb("azb", [128, 512], F32)
        aztb = sb("aztb", [128, 512], F32)
        tt1 = sb("tt1", [128, 512], F32)
        tt2 = sb("tt2", [128, 512], F32)
        i2b = sb("i2b", [128, 512], F32)
        ub = sb("ub", [128, 128], BF16)
        tbf = sb("tbf", [128, 128], F32)
        m3 = sb("m3", [128, 8], F32)
        sm = sb("sm", [128, 8], F32)
        reca = sb("reca", [128, 2], F32)
        recb = sb("recb", [128, 2], F32)
        scal = sb("scal", [1, 8], F32)
        scl2 = sb("scl2", [128, 2], F32)
        ones1 = sb("ones1", [1, 128], F32)
        onesc = sb("onesc", [128, 2], F32)
        macc = sb("macc", [128, 2], F32)
        ocst = sb("ocst", [128, 256], F32)
        dbgs = sb("dbgs", [1, 8], F32)

        pa = psum("pa", [128, 512])
        pb = psum("pb", [128, 512])
        pc = psum("pc", [128, 512])
        pt0 = psum("pt0", [128, 64])
        pt1 = psum("pt1", [128, 64])
        ptrA = psum("ptrA", [128, 128], BF16)
        ptrB = psum("ptrB", [128, 128], BF16)
        ptf = psum("ptf", [128, 128])

        s_bar = sem("s_bar")
        s_ld0 = sem("s_ld0")
        s_ldg = sem("s_ldg")
        s_tl = sem("s_tl")
        s_pe = sem("s_pe")
        s_dv = sem("s_dv")
        s_ac = sem("s_ac")
        s_nm = sem("s_nm")
        s_chv = sem("s_chv")
        s_st0 = sem("s_st0")
        s_st1 = sem("s_st1")
        blk = ctx.enter_context(nc.Block())

        LDG_TOTAL = 16 * (1 + 3 + 8 + 8 + 16)

        @blk.sync
        def _(sync):
            sync.dma_start(out=idf[:, :], in_=iden[:, :]).then_inc(s_ld0, 16)
            sync.dma_start(out=i2b[:, 0:256], in_=i256[0:128, :]).then_inc(s_ld0, 16)
            sync.dma_start(out=i2b[:, 256:512], in_=i256[128:256, :]).then_inc(s_ld0, 16)
            for b in range(NBLK if stages >= 4 else 0):
                if b == 0:
                    sync.wait_ge(s_pe, 128)
                else:
                    sync.wait_ge(s_pe, PE4 + 28 * (b - 1) + 12)
                for k in range(8):
                    sync.dma_start(
                        out=ar2[:, k * 2048:(k + 1) * 2048],
                        in_=ln0[b * 2048:(b + 1) * 2048, k * 128:(k + 1) * 128],
                        transpose=True,
                    ).then_inc(s_tl, 16)

        @blk.gpsimd
        def _(g):
            g.dma_start(out=idb[:, :], in_=iden[:, :]).then_inc(s_ldg, 16)
            for d in range(3):
                g.dma_start(out=topb[:, d * 128:(d + 1) * 128],
                            in_=tope[d, :, :]).then_inc(s_ldg, 16)
            for k in range(8):
                g.dma_start(out=qkvw_b[:, k * 1024:(k + 1) * 1024],
                            in_=qkvw[k * 128:(k + 1) * 128, :]).then_inc(s_ldg, 16)
            for k in range(8):
                g.dma_start(out=qkvwh_b[:, k * 192:(k + 1) * 192],
                            in_=qkvwh[k * 128:(k + 1) * 128, :]).then_inc(s_ldg, 16)
            for i in range(16):
                g.dma_start(out=ar2[:, i * 1024:(i + 1) * 1024],
                            in_=xlf[i * 128:(i + 1) * 128, :]).then_inc(s_ldg, 16)
            if stages >= 7:
                for t in range(NTT):
                    p = t % 2
                    g.wait_ge(s_dv, DV7 + t + 1)
                    g.dma_start(out=outc[t * 128:(t + 1) * 128, :],
                                in_=ocst[:, p * 128: p * 128 + 64]
                                ).then_inc(s_st0 if p == 0 else s_st1, 16)
            else:
                for t in range(NTT):
                    g.dma_start(out=outc[t * 128:(t + 1) * 128, :],
                                in_=ocst[:, 0:64]).then_inc(s_st1, 16)
            if stages >= 2:
                g.wait_ge(s_dv, DV2D + 1)
            g.dma_start(out=bdbg[:, :], in_=dbgs[:, :]).then_inc(s_st0, 16)
            if stages >= 7:
                g.wait_ge(s_st0, 16 * (NTT // 2 + 1))
                g.wait_ge(s_st1, 16 * (NTT // 2))
            else:
                g.wait_ge(s_st0, 16)
                g.wait_ge(s_st1, 16 * NTT)

        @blk.tensor
        def _(pe):
            pe.wait_ge(s_ldg, LDG_TOTAL)
            pe.wait_ge(s_ld0, 48)
            for j in range(8 * nxl if sub >= 0 else 0):
                i, k = divmod(j, 8)
                sl = j % 2
                if j >= 2:
                    pe.wait_ge(s_dv, DV1A + j - 1)
                pe.transpose((ptrA if sl == 0 else ptrB)[:, :],
                             ar2[:, (i + xlo) * 1024 + k * 128:
                                 (i + xlo) * 1024 + (k + 1) * 128],
                             idb[:, :]).then_inc(s_pe, 1)
            if sub >= 1:
                pe.wait_ge(s_dv, DV1A + 128)
            for gidx in range(32 if sub >= 1 else 0):
                oc, fg = divmod(gidx, 4)
                if gidx >= 1:
                    pe.wait_ge(s_dv, DV1B + gidx)
                for kc in range(8):
                    mm = pe.matmul(pa[:, :],
                                   qkvw_b[0:128, kc * 1024 + oc * 128: kc * 1024 + (oc + 1) * 128],
                                   ar1[:, kc * 2048 + fg * 512: kc * 2048 + (fg + 1) * 512],
                                   start=(kc == 0), stop=(kc == 7))
                mm.then_inc(s_pe, 1)
            for sl in range(2 if sub >= 2 else 0):
                pe.wait_ge(s_dv, DV1C + sl)
                for kc in range(8):
                    mm = pe.matmul(pb[0:64, 0:256],
                                   qkvwh_b[0:128, kc * 192 + sl * 64: kc * 192 + (sl + 1) * 64],
                                   ar1[:, kc * 2048 + 0: kc * 2048 + 256],
                                   start=(kc == 0), stop=(kc == 7))
                mm.then_inc(s_pe, 1)
            pe.nop().then_inc(s_bar, 1)
            pe.wait_ge(s_bar, 3 * 1)
            if stages <= 1:
                return
            for pair in range(64):
                b, h = divmod(pair, 8)
                chh = h // 2
                off = 64 * (h % 2)
                if pair >= 1:
                    pe.wait_ge(s_ac, AC2 + 2 * pair)
                for qc in range(2):
                    mm = pe.matmul(pa[:, qc * 256:(qc + 1) * 256],
                                   qlkl[off:off + 64,
                                        chh * 2048 + b * 256 + qc * 128:
                                        chh * 2048 + b * 256 + (qc + 1) * 128],
                                   qlkl[off:off + 64,
                                        (4 + chh) * 2048 + b * 256:(4 + chh) * 2048 + (b + 1) * 256],
                                   start=True, stop=True)
                mm.then_inc(s_pe, 1)
                pe.wait_ge(s_dv, DV2A + 2 * pair + 1)
                if pair >= 1:
                    pe.wait_ge(s_dv, DV2A + 2 * pair)
                for qc in range(2):
                    mm = pe.matmul(pc[0:1, 0:256], onesc[:, 0:1],
                                   fscr[:, qc * 256:(qc + 1) * 256],
                                   start=(qc == 0), stop=(qc == 1))
                mm.then_inc(s_pe, 1)
            for qc in range(2):
                mm = pe.matmul(pb[:, qc * 256:(qc + 1) * 256],
                               qlhf[0:64, qc * 128:(qc + 1) * 128],
                               qlhf[0:64, 256:512], start=True, stop=True)
            mm.then_inc(s_pe, 1)
            pe.wait_ge(s_dv, DV2B + 1)
            pe.transpose(ptf[0:1, 0:128], macc[:, 0:1], idf[:, :]).then_inc(s_pe, 1)
            pe.wait_ge(s_dv, DV2C + 1)
            pe.matmul(pc[0:128, 0:1], ones1[0:1, :], scal[0:1, 2:3],
                      start=True, stop=True).then_inc(s_pe, 1)
            pe.nop().then_inc(s_bar, 1)
            pe.wait_ge(s_bar, 3 * 2)
            if stages <= 2:
                return
            for j in range(4):
                ic, kc = divmod(j, 2)
                if j >= 1:
                    pe.wait_ge(s_dv, DV3A + j)
                pe.transpose(ptf[:, :],
                             amat[:, ic * 256 + kc * 128: ic * 256 + (kc + 1) * 128],
                             idf[:, :]).then_inc(s_pe, 1)
            for it in range(6):
                dvi = DV3I + 16 * it
                pe.wait_ge(s_dv, (DV3B + 2) if it == 0 else (DV3I + 16 * (it - 1) + 12))
                for ic in range(2):
                    for kc in range(2):
                        mm = pe.matmul(pa[:, ic * 256:(ic + 1) * 256],
                                       atm[:, kc * 256 + ic * 128: kc * 256 + (ic + 1) * 128],
                                       zb[:, kc * 256:(kc + 1) * 256],
                                       start=(kc == 0), stop=(kc == 1))
                mm.then_inc(s_pe, 1)
                for j in range(4):
                    ic, kc = divmod(j, 2)
                    pe.wait_ge(s_dv, dvi + 2 + j)
                    pe.transpose(ptf[:, :],
                                 azb[:, ic * 256 + kc * 128: ic * 256 + (kc + 1) * 128],
                                 idf[:, :]).then_inc(s_pe, 1)
                pe.wait_ge(s_dv, dvi + 7)
                for ic in range(2):
                    for kc in range(2):
                        mm = pe.matmul(pb[:, ic * 256:(ic + 1) * 256],
                                       aztb[:, kc * 256 + ic * 128: kc * 256 + (ic + 1) * 128],
                                       tt1[:, kc * 256:(kc + 1) * 256],
                                       start=(kc == 0), stop=(kc == 1))
                mm.then_inc(s_pe, 1)
                pe.wait_ge(s_dv, dvi + 9)
                for ic in range(2):
                    for kc in range(2):
                        mm = pe.matmul(pb[:, ic * 256:(ic + 1) * 256],
                                       aztb[:, kc * 256 + ic * 128: kc * 256 + (ic + 1) * 128],
                                       tt1[:, kc * 256:(kc + 1) * 256],
                                       start=(kc == 0), stop=(kc == 1))
                mm.then_inc(s_pe, 1)
                pe.wait_ge(s_dv, dvi + 11)
                for ic in range(2):
                    for kc in range(2):
                        mm = pe.matmul(pa[:, ic * 256:(ic + 1) * 256],
                                       ztb[:, kc * 256 + ic * 128: kc * 256 + (ic + 1) * 128],
                                       tt1[:, kc * 256:(kc + 1) * 256],
                                       start=(kc == 0), stop=(kc == 1))
                mm.then_inc(s_pe, 1)
                for j in range(4):
                    ic, kc = divmod(j, 2)
                    pe.wait_ge(s_dv, dvi + 12 + j)
                    pe.transpose(ptf[:, :],
                                 zb[:, ic * 256 + kc * 128: ic * 256 + (kc + 1) * 128],
                                 idf[:, :]).then_inc(s_pe, 1)
            pe.nop().then_inc(s_bar, 1)
            pe.wait_ge(s_bar, 3 * 3)
            if stages <= 3:
                return
            for b in range(NBLK):
                pe.wait_ge(s_tl, 16 * 8 * (b + 1))
                for gidx in range(12):
                    sl, fg = divmod(gidx, 4)
                    pe.wait_ge(s_dv, DV4 + 28 * b + gidx)
                    for kc in range(8):
                        mm = pe.matmul(pb[0:64, 0:512],
                                       qkvwh_b[0:128, kc * 192 + sl * 64: kc * 192 + (sl + 1) * 64],
                                       ar2[:, kc * 2048 + fg * 512: kc * 2048 + (fg + 1) * 512],
                                       start=(kc == 0), stop=(kc == 7))
                    mm.then_inc(s_pe, 1)
                pe.wait_ge(s_dv, DV4 + 28 * b + 12)
                for tt in range(TPB):
                    t = b * TPB + tt
                    sl2 = t % 2
                    if tt >= 2 or b >= 1:
                        pe.wait_ge(s_dv, DV4 + 28 * b + 12 + tt - 1 if tt >= 2
                                   else DV4 + 28 * b)
                    pe.transpose((ptrA if sl2 == 0 else ptrB)[:, 0:64],
                                 vnb[0:64, t * 128:(t + 1) * 128],
                                 idb[0:64, 0:64]).then_inc(s_pe, 1)
            pe.nop().then_inc(s_bar, 1)
            pe.wait_ge(s_bar, 3 * 4)
            if stages <= 4:
                return
            for t in range(NTT):
                if t >= 1:
                    pe.wait_ge(s_ac, AC5 + t)
                pe.matmul(pa[:, 0:256], qkn[0:64, t * 128:(t + 1) * 128],
                          qlh[0:64, 256:512], start=True, stop=True).then_inc(s_pe, 1)
                pe.wait_ge(s_dv, DV5 + 2 * t + 1)
                for qc in range(2):
                    mm = pe.transpose((ptrA if qc == 0 else ptrB)[:, :],
                                      bscr[:, qc * 128:(qc + 1) * 128], idb[:, :])
                mm.then_inc(s_pe, 1)
            pe.nop().then_inc(s_bar, 1)
            pe.wait_ge(s_bar, 3 * 5)
            if stages <= 5:
                return
            for t in range(NTT):
                if t >= 1:
                    pe.wait_ge(s_dv, DV6A + t)
                for qc in range(2):
                    mm = pe.matmul(pc[:, qc * 128:(qc + 1) * 128],
                                   qlh[0:64, qc * 128:(qc + 1) * 128],
                                   qkvw_b[0:64, t * 128:(t + 1) * 128],
                                   start=True, stop=True)
                mm.then_inc(s_pe, 1)
            pe.wait_ge(s_dv, DV6B + 1)
            for t in range(NTT):
                pe.wait_ge(s_ac, AC6 + 2 * t)
                for qc in range(2):
                    mm = pe.matmul(pc[:, qc * 128:(qc + 1) * 128],
                                   qlh[0:64, qc * 128:(qc + 1) * 128],
                                   qkvw_b[0:64, t * 128:(t + 1) * 128],
                                   start=True, stop=True)
                mm.then_inc(s_pe, 1)
                pe.wait_ge(s_dv, DV6C + 2 * t + 1)
                for qc in range(2):
                    mm = pe.transpose((ptrA if qc == 0 else ptrB)[:, :],
                                      bscr[:, qc * 128:(qc + 1) * 128], idb[:, :])
                mm.then_inc(s_pe, 1)
                pe.wait_ge(s_dv, DV6C + 2 * t + 2)
                for qc in range(2):
                    mm = pe.matmul(pt0[:, :] if qc == 0 else pt1[:, :],
                                   bscr[:, 256 + qc * 128: 256 + (qc + 1) * 128],
                                   vnt[:, t * 64:(t + 1) * 64],
                                   start=(t == 0), stop=(t == NTT - 1))
            mm.then_inc(s_pe, 1)
            pe.wait_ge(s_dv, DV6D)
            for ic in range(2):
                for kc in range(2):
                    mm = pe.matmul(pb[:, ic * 64:(ic + 1) * 64],
                                   ztb[:, kc * 256 + ic * 128: kc * 256 + (ic + 1) * 128],
                                   tbf[:, kc * 64:(kc + 1) * 64],
                                   start=(kc == 0), stop=(kc == 1))
            mm.then_inc(s_pe, 1)
            pe.nop().then_inc(s_bar, 1)
            pe.wait_ge(s_bar, 3 * 6)
            if stages <= 6:
                return
            for t in range(NTT):
                p = t % 2
                bank = pa if p == 0 else pb
                if t >= 2:
                    pe.wait_ge(s_dv, DV7 + t - 1)
                pe.matmul(bank[:, 0:64],
                          ar1[:, t * 128:(t + 1) * 128], ub[:, 0:64],
                          start=True, stop=False)
                pe.matmul(bank[:, 0:64],
                          ar1[:, 8192 + t * 128: 8192 + (t + 1) * 128], ub[:, 64:128],
                          start=False, stop=False)
                deltas = [dd for dd in (-1, 0, 1) if 0 <= t + dd < NTT]
                for di, dd in enumerate(deltas):
                    mm = pe.matmul(bank[:, 0:64],
                                   topb[:, (dd + 1) * 128:(dd + 2) * 128],
                                   vnt[:, (t + dd) * 64:(t + dd + 1) * 64],
                                   start=False, stop=(di == len(deltas) - 1))
                mm.then_inc(s_pe, 1)

        @blk.vector
        def _(v):
            chv = Chain(v, s_chv)
            chv(v.memset(ones1[:, :], 1.0))
            chv(v.memset(onesc[:, :], 1.0))
            chv(v.memset(macc[:, :], 0.0))
            chv(v.memset(scal[0:1, :], 0.0))
            chv(v.memset(m3[:, 0:8], 0.0))
            chv(v.tensor_scalar(m3[:, 0:2], m3[:, 0:2], 0.0, -1e30, ALU.mult, ALU.add))
            for j in range(8 * nxl if sub >= 0 else 0):
                i, k = divmod(j, 8)
                sl = j % 2
                v.wait_ge(s_pe, PE1A + j + 1)
                v.tensor_copy(ar1[:, k * 2048 + i * 128: k * 2048 + (i + 1) * 128],
                              (ptrA if sl == 0 else ptrB)[:, :]).then_inc(s_dv, 1)
            for gidx in range(32 if sub >= 1 else 0):
                oc, fg = divmod(gidx, 4)
                v.wait_ge(s_pe, PE1B + gidx + 1)
                dst = qlkl[:, oc * 2048 + fg * 512: oc * 2048 + (fg + 1) * 512]
                if oc < 4:
                    v.tensor_scalar(dst, pa[:, :], 0.125, None, ALU.mult).then_inc(s_dv, 1)
                else:
                    v.tensor_copy(dst, pa[:, :]).then_inc(s_dv, 1)
            for sl in range(2 if sub >= 2 else 0):
                v.wait_ge(s_pe, PE1C + sl + 1)
                if sl == 0:
                    chv(v.tensor_scalar(qlhf[0:64, 0:256], pb[0:64, 0:256],
                                        0.125, None, ALU.mult))
                    v.tensor_copy(qlh[0:64, 0:256], qlhf[0:64, 0:256]).then_inc(s_dv, 1)
                else:
                    chv(v.tensor_copy(qlhf[0:64, 256:512], pb[0:64, 0:256]))
                    v.tensor_copy(qlh[0:64, 256:512], qlhf[0:64, 256:512]).then_inc(s_dv, 1)
            v.engine_nop().then_inc(s_bar, 1)
            v.wait_ge(s_bar, 3 * 1)
            if stages <= 1:
                return
            for pair in range(64):
                v.wait_ge(s_pe, PE2A + 2 * pair + 1)
                chv(v.tensor_reduce(sm[:, 0:1], pa[:, 0:256], AX.X, ALU.max))
                chv(v.tensor_reduce(sm[:, 1:2], pa[:, 256:512], AX.X, ALU.max))
                chv(v.tensor_scalar(sm[:, 2:4], sm[:, 0:2], -1.0, None, ALU.mult))
                v.engine_nop().then_inc(s_nm, 1)
                v.wait_ge(s_ac, AC2 + 2 * pair + 2)
                chv(v.tensor_reduce(sm[:, 4:5], fscr[:, 0:256], AX.X, ALU.add))
                chv(v.tensor_reduce(sm[:, 5:6], fscr[:, 256:512], AX.X, ALU.add))
                chv(v.reciprocal(reca[:, 0:1], sm[:, 4:5]))
                chv(v.reciprocal(recb[:, 0:1], sm[:, 5:6]))
                chv(v.tensor_scalar(fscr[:, 0:256], fscr[:, 0:256], reca[:, 0:1],
                                    None, ALU.mult))
                chv(v.tensor_scalar(fscr[:, 256:512], fscr[:, 256:512], recb[:, 0:1],
                                    None, ALU.mult))
                chv(v.tensor_reduce(sm[:, 6:7], fscr[:, 0:256], AX.X, ALU.add))
                chv(v.tensor_reduce(sm[:, 7:8], fscr[:, 256:512], AX.X, ALU.add))
                chv(v.tensor_tensor(macc[:, 0:1], macc[:, 0:1], sm[:, 6:7], ALU.max))
                chv(v.tensor_tensor(macc[:, 0:1], macc[:, 0:1], sm[:, 7:8], ALU.max))
                v.engine_nop().then_inc(s_dv, 1)
                v.wait_ge(s_pe, PE2A + 2 * pair + 2)
                chv(v.tensor_reduce(sm[0:1, 0:1], pc[0:1, 0:256], AX.X, ALU.max))
                chv(v.tensor_tensor(scal[0:1, 1:2], scal[0:1, 1:2], sm[0:1, 0:1], ALU.max))
                v.engine_nop().then_inc(s_dv, 1)
            v.wait_ge(s_pe, PE2B + 1)
            chv(v.tensor_reduce(sm[:, 0:1], pb[:, 0:256], AX.X, ALU.max))
            chv(v.tensor_reduce(sm[:, 1:2], pb[:, 256:512], AX.X, ALU.max))
            chv(v.tensor_scalar(sm[:, 2:4], sm[:, 0:2], -1.0, None, ALU.mult))
            v.engine_nop().then_inc(s_nm, 1)
            v.wait_ge(s_ac, AC2H + 2)
            chv(v.tensor_reduce(sm[:, 4:5], fscr[:, 0:256], AX.X, ALU.add))
            chv(v.tensor_reduce(sm[:, 5:6], fscr[:, 256:512], AX.X, ALU.add))
            chv(v.reciprocal(reca[:, 0:1], sm[:, 4:5]))
            chv(v.reciprocal(recb[:, 0:1], sm[:, 5:6]))
            chv(v.tensor_scalar(amat[:, 0:256], fscr[:, 0:256], reca[:, 0:1], None, ALU.mult))
            chv(v.tensor_scalar(amat[:, 256:512], fscr[:, 256:512], recb[:, 0:1],
                                None, ALU.mult))
            v.engine_nop().then_inc(s_dv, 1)
            v.wait_ge(s_pe, PE2C + 1)
            chv(v.tensor_reduce(scal[0:1, 0:1], ptf[0:1, 0:128], AX.X, ALU.max))
            chv(v.tensor_tensor(scal[0:1, 4:5], scal[0:1, 0:1], scal[0:1, 1:2], ALU.mult))
            chv(v.reciprocal(scal[0:1, 2:3], scal[0:1, 4:5]))
            chv(v.tensor_copy(dbgs[0:1, 0:8], scal[0:1, 0:8]))
            v.engine_nop().then_inc(s_dv, 1)
            v.wait_ge(s_pe, PE2D + 1)
            chv(v.tensor_copy(scl2[:, 0:1], pc[0:128, 0:1]))
            v.engine_nop().then_inc(s_dv, 1)
            v.engine_nop().then_inc(s_bar, 1)
            v.wait_ge(s_bar, 3 * 2)
            if stages <= 2:
                return
            for j in range(4):
                ic, kc = divmod(j, 2)
                v.wait_ge(s_pe, PE3A + j + 1)
                chv(v.tensor_copy(atm[:, kc * 256 + ic * 128: kc * 256 + (ic + 1) * 128],
                                  ptf[:, :]))
                v.engine_nop().then_inc(s_dv, 1)
            v.tensor_scalar(zb[:, :], atm[:, :], scl2[:, 0:1], None,
                            ALU.mult).then_inc(s_dv, 1)
            v.tensor_scalar(ztb[:, :], amat[:, :], scl2[:, 0:1], None,
                            ALU.mult).then_inc(s_dv, 1)
            for it in range(6):
                pei = PE3I + 12 * it
                dvi = DV3I + 16 * it
                v.wait_ge(s_pe, pei + 1)
                for ic in range(2):
                    chv(v.tensor_copy(azb[:, ic * 256:(ic + 1) * 256],
                                      pa[:, ic * 256:(ic + 1) * 256]))
                    v.engine_nop().then_inc(s_dv, 1)
                for j in range(4):
                    v.wait_ge(s_pe, pei + 1 + j + 1)
                    ic, kc = divmod(j, 2)
                    v.tensor_copy(aztb[:, kc * 256 + ic * 128: kc * 256 + (ic + 1) * 128],
                                  ptf[:, :]).then_inc(s_dv, 1)
                v.scalar_tensor_tensor(tt1[:, :], i2b[:, :], 7.0, azb[:, :],
                                       ALU.mult, ALU.subtract).then_inc(s_dv, 1)
                v.wait_ge(s_pe, pei + 6)
                chv(v.tensor_copy(tt2[:, :], pb[:, :]))
                v.engine_nop().then_inc(s_dv, 1)
                v.scalar_tensor_tensor(tt1[:, :], i2b[:, :], 15.0, tt2[:, :],
                                       ALU.mult, ALU.subtract).then_inc(s_dv, 1)
                v.wait_ge(s_pe, pei + 7)
                chv(v.tensor_copy(tt2[:, :], pb[:, :]))
                v.engine_nop().then_inc(s_dv, 1)
                v.scalar_tensor_tensor(tt1[:, :], i2b[:, :], 13.0, tt2[:, :],
                                       ALU.mult, ALU.subtract).then_inc(s_dv, 1)
                v.wait_ge(s_pe, pei + 8)
                v.tensor_scalar(zb[:, :], pa[:, :], 0.25, None, ALU.mult).then_inc(s_dv, 1)
                for j in range(4):
                    v.wait_ge(s_pe, pei + 8 + j + 1)
                    ic, kc = divmod(j, 2)
                    v.tensor_copy(ztb[:, kc * 256 + ic * 128: kc * 256 + (ic + 1) * 128],
                                  ptf[:, :]).then_inc(s_dv, 1)
            v.engine_nop().then_inc(s_bar, 1)
            v.wait_ge(s_bar, 3 * 3)
            if stages <= 3:
                return
            for b in range(NBLK):
                for gidx in range(12):
                    sl, fg = divmod(gidx, 4)
                    v.wait_ge(s_pe, PE4 + 28 * b + gidx + 1)
                    col = b * 2048 + fg * 512
                    if sl == 0:
                        v.tensor_scalar(qkn[0:64, col: col + 512], pb[0:64, 0:512],
                                        0.125, None, ALU.mult).then_inc(s_dv, 1)
                    elif sl == 1:
                        v.tensor_copy(qkvw_b[0:64, col: col + 512],
                                      pb[0:64, 0:512]).then_inc(s_dv, 1)
                    else:
                        v.tensor_copy(vnb[0:64, col: col + 512],
                                      pb[0:64, 0:512]).then_inc(s_dv, 1)
                for tt in range(TPB):
                    t = b * TPB + tt
                    sl2 = t % 2
                    v.wait_ge(s_pe, PE4 + 28 * b + 12 + tt + 1)
                    v.tensor_copy(vnt[:, t * 64:(t + 1) * 64],
                                  (ptrA if sl2 == 0 else ptrB)[:, 0:64]).then_inc(s_dv, 1)
            v.engine_nop().then_inc(s_bar, 1)
            v.wait_ge(s_bar, 3 * 4)
            if stages <= 4:
                return
            for t in range(NTT):
                v.wait_ge(s_pe, PE5 + 2 * t + 1)
                chv(v.tensor_reduce(sm[:, 0:1], pa[:, 0:256], AX.X, ALU.max))
                chv(v.tensor_scalar(sm[:, 2:3], sm[:, 0:1], -1.0, None, ALU.mult))
                v.engine_nop().then_inc(s_nm, 1)
                v.wait_ge(s_ac, AC5 + t + 1)
                chv(v.tensor_reduce(sm[:, 4:5], fscr[:, 0:256], AX.X, ALU.add))
                chv(v.reciprocal(reca[:, 0:1], sm[:, 4:5]))
                v.tensor_scalar(bscr[:, 0:256], fscr[:, 0:256], reca[:, 0:1], None,
                                ALU.mult).then_inc(s_dv, 1)
                v.wait_ge(s_pe, PE5 + 2 * t + 2)
                v.tensor_copy(ar1[:, t * 128:(t + 1) * 128], ptrA[:, :])
                v.tensor_copy(ar1[:, 8192 + t * 128: 8192 + (t + 1) * 128],
                              ptrB[:, :]).then_inc(s_dv, 1)
            v.engine_nop().then_inc(s_bar, 1)
            v.wait_ge(s_bar, 3 * 5)
            if stages <= 5:
                return
            for t in range(NTT):
                v.wait_ge(s_pe, PE6A + t + 1)
                chv(v.tensor_reduce(sm[:, 0:1], pc[:, 0:128], AX.X, ALU.max))
                chv(v.tensor_reduce(sm[:, 1:2], pc[:, 128:256], AX.X, ALU.max))
                chv(v.tensor_tensor(m3[:, 0:1], m3[:, 0:1], sm[:, 0:1], ALU.max))
                chv(v.tensor_tensor(m3[:, 1:2], m3[:, 1:2], sm[:, 1:2], ALU.max))
                v.engine_nop().then_inc(s_dv, 1)
            chv(v.tensor_scalar(m3[:, 2:4], m3[:, 0:2], -1.0, None, ALU.mult))
            chv(v.memset(m3[:, 4:6], 0.0))
            v.engine_nop().then_inc(s_nm, 1)
            v.engine_nop().then_inc(s_dv, 1)
            for t in range(NTT):
                v.wait_ge(s_ac, AC6 + 2 * t + 2)
                chv(v.tensor_reduce(sm[:, 0:1], fscr[:, 0:128], AX.X, ALU.add))
                chv(v.tensor_reduce(sm[:, 1:2], fscr[:, 128:256], AX.X, ALU.add))
                chv(v.tensor_tensor(m3[:, 4:5], m3[:, 4:5], sm[:, 0:1], ALU.add))
                chv(v.tensor_tensor(m3[:, 5:6], m3[:, 5:6], sm[:, 1:2], ALU.add))
                v.tensor_copy(bscr[:, 0:256], fscr[:, 0:256]).then_inc(s_dv, 1)
                v.wait_ge(s_pe, PE6B + 2 * t + 2)
                v.tensor_copy(bscr[:, 256:384], ptrA[:, :])
                v.tensor_copy(bscr[:, 384:512], ptrB[:, :]).then_inc(s_dv, 1)
            v.wait_ge(s_pe, PE6B + 2 * NTT + 1)
            chv(v.tensor_copy(tbf[:, 0:64], pt0[:, :]))
            chv(v.tensor_copy(tbf[:, 64:128], pt1[:, :]))
            chv(v.tensor_copy(reca[:, 0:1], m3[:, 4:5]))
            chv(v.reciprocal(recb[:, 0:1], reca[:, 0:1]))
            chv(v.tensor_scalar(tbf[:, 0:64], tbf[:, 0:64], recb[:, 0:1], None, ALU.mult))
            chv(v.tensor_copy(reca[:, 0:1], m3[:, 5:6]))
            chv(v.reciprocal(recb[:, 0:1], reca[:, 0:1]))
            chv(v.tensor_scalar(tbf[:, 64:128], tbf[:, 64:128], recb[:, 0:1],
                                None, ALU.mult))
            v.engine_nop().then_inc(s_dv, 1)
            v.wait_ge(s_pe, PE6C + 1)
            v.tensor_copy(ub[:, 0:64], pb[:, 0:64])
            v.tensor_copy(ub[:, 64:128], pb[:, 64:128]).then_inc(s_dv, 1)
            v.engine_nop().then_inc(s_bar, 1)
            v.wait_ge(s_bar, 3 * 6)
            if stages <= 6:
                return
            for t in range(NTT):
                p = t % 2
                bank = pa if p == 0 else pb
                v.wait_ge(s_pe, PE7 + t + 1)
                if t >= 2:
                    v.wait_ge(s_st0 if p == 0 else s_st1, 16 * (t // 2))
                v.tensor_copy(ocst[:, p * 128: p * 128 + 64],
                              bank[:, 0:64]).then_inc(s_dv, 1)

        @blk.scalar
        def _(act):
            act.nop().then_inc(s_bar, 1)
            act.wait_ge(s_bar, 3 * 1)
            if stages <= 1:
                return
            for pair in range(64):
                act.wait_ge(s_nm, pair + 1)
                if pair >= 1:
                    act.wait_ge(s_dv, DV2A + 2 * pair - 1)
                for qc in range(2):
                    act.activation(fscr[:, qc * 256:(qc + 1) * 256],
                                   pa[:, qc * 256:(qc + 1) * 256], AF.Exp,
                                   bias=sm[:, 2 + qc: 3 + qc]).then_inc(s_ac, 1)
            act.wait_ge(s_nm, 65)
            act.wait_ge(s_dv, DV2A + 128 - 1)
            for qc in range(2):
                act.activation(fscr[:, qc * 256:(qc + 1) * 256],
                               pb[:, qc * 256:(qc + 1) * 256], AF.Exp,
                               bias=sm[:, 2 + qc: 3 + qc]).then_inc(s_ac, 1)
            act.nop().then_inc(s_bar, 1)
            act.wait_ge(s_bar, 3 * 2)
            if stages <= 2:
                return
            act.nop().then_inc(s_bar, 1)
            act.wait_ge(s_bar, 3 * 3)
            if stages <= 3:
                return
            act.nop().then_inc(s_bar, 1)
            act.wait_ge(s_bar, 3 * 4)
            if stages <= 4:
                return
            for t in range(NTT):
                act.wait_ge(s_nm, 65 + t + 1)
                if t >= 1:
                    act.wait_ge(s_dv, DV5 + 2 * t - 1)
                act.activation(fscr[:, 0:256], pa[:, 0:256], AF.Exp,
                               bias=sm[:, 2:3]).then_inc(s_ac, 1)
            act.nop().then_inc(s_bar, 1)
            act.wait_ge(s_bar, 3 * 5)
            if stages <= 5:
                return
            act.wait_ge(s_nm, 130)
            for t in range(NTT):
                act.wait_ge(s_pe, PE6B + 2 * t + 1)
                if t >= 1:
                    act.wait_ge(s_dv, DV6C + 2 * t - 1)
                for qc in range(2):
                    act.activation(fscr[:, qc * 128:(qc + 1) * 128],
                                   pc[:, qc * 128:(qc + 1) * 128], AF.Exp,
                                   bias=m3[:, 2 + qc: 3 + qc]).then_inc(s_ac, 1)
            act.nop().then_inc(s_bar, 1)

    return nc


def host_rB(inputs, rA_outs):
    qkv_w = inputs["L0_attn_qkv_w"]
    conv_w = inputs["L0_attn_conv_w"]
    xlf = np.zeros((2048, D), np.float32)
    for c in range(NCORES):
        xp = np.asarray(rA_outs[c]["xlp"])
        for b in range(B):
            xlf[b * 256 + 32 * c: b * 256 + 32 * (c + 1), :] = xp[b]
    ln0 = np.concatenate([np.asarray(rA_outs[c]["ln1sp"])[0] for c in range(NCORES)],
                         axis=0)
    iden = np.eye(128, dtype=np.float32)
    i256 = np.eye(256, dtype=np.float32)
    maps = []
    for h in range(NCORES):
        cols = (list(range(h * 64, (h + 1) * 64))
                + list(range(512 + h * 64, 512 + (h + 1) * 64))
                + list(range(1024 + h * 64, 1024 + (h + 1) * 64)))
        qkvwh = np.ascontiguousarray(qkv_w[:, cols])
        tope = np.zeros((3, 128, 128), np.float32)
        w = conv_w[h, 0, :, 0]
        for t in range(128):
            for dlt, base in ((0, 16 - 128), (1, 16), (2, 16 + 128)):
                for k in range(128):
                    s = k - t + base
                    if 0 <= s < CONV_K:
                        tope[dlt, k, t] = w[s]
        maps.append({
            "xlf": xlf,
            "ln0": ln0,
            "qkvw": np.ascontiguousarray(qkv_w[:, 0:1024]),
            "qkvwh": qkvwh,
            "tope": tope,
            "iden": iden,
            "i256": i256,
        })
    return maps




class Prog:
    def __init__(self, nc, blk, ctx):
        self.nc, self.blk, self.ctx = nc, blk, ctx
        self.sems = {}

    def sem(self, *names):
        for n in names:
            self.sems[n] = [self.ctx.enter_context(self.nc.semaphore(n)), 0]

    def inc(self, inst, name, k=1):
        h = self.sems[name]
        inst.then_inc(h[0], k)
        h[1] += k
        return h[1]

    def n(self, name):
        return self.sems[name][1]

    def wait(self, eng, name, val=None):
        h = self.sems[name]
        eng.wait_ge(h[0], h[1] if val is None else val)



def build_rC():
    nc = _bass()
    ocsl = nc.dram_tensor("ocsl", [TOK, 512], F32, kind="ExternalInput")
    outw = nc.dram_tensor("outw", [512, D], F32, kind="ExternalInput")
    obvi = nc.dram_tensor("obvi", [128, 8], F32, kind="ExternalInput")
    wq = nc.dram_tensor("wq", [D, 256], F32, kind="ExternalInput")
    wk = nc.dram_tensor("wk", [D, 256], F32, kind="ExternalInput")
    ln0sl = nc.dram_tensor("ln0sl", [TOK, D], BF16, kind="ExternalInput")
    iden = nc.dram_tensor("iden", [128, 128], F32, kind="ExternalInput")

    xg0tm = nc.dram_tensor("xg0tm", [TOK, D], BF16, kind="ExternalOutput")
    qesl = nc.dram_tensor("qesl", [TOK, 256], F32, kind="ExternalOutput")
    kesl = nc.dram_tensor("kesl", [TOK, 256], F32, kind="ExternalOutput")

    NTT = TOK // 128
    with ExitStack() as ctx:
        def sb(name, shape, dt=F32):
            return ctx.enter_context(nc.sbuf_tensor(name, shape, dt))

        def psum(name, shape, dt=F32):
            return ctx.enter_context(nc.psum_tensor(name, shape, dt))

        ocb = sb("ocb", [128, NTT * 512], BF16)
        ocf = sb("ocf", [128, 4 * TOK], BF16)
        owb = sb("owb", [128, 4 * D], BF16)
        wqb = sb("wqb", [128, 8 * 256], BF16)
        wkb = sb("wkb", [128, 8 * 256], BF16)
        lnf = sb("lnf", [128, 8 * TOK], BF16)
        encf = sb("encf", [128, 8 * TOK], BF16)
        xgf = sb("xgf", [128, TOK], F32)
        xgb = sb("xgb", [128, 8 * TOK], BF16)
        obv = sb("obv", [128, 8], F32)
        idb = sb("idb", [128, 128], BF16)
        stg = sb("stg", [128, 2 * D], BF16)
        qst = sb("qst", [128, 4 * 256], F32)
        pa = psum("pa", [128, 512])
        pb = psum("pb", [128, 512])
        ptA = psum("ptA", [128, 128], BF16)
        ptB = psum("ptB", [128, 128], BF16)
        blk = ctx.enter_context(nc.Block())
        P = Prog(nc, blk, ctx)
        P.sem("lg", "ld", "tl", "pe", "dv", "ac", "stA", "stB", "chv")

        def G(f):
            blk.gpsimd(f)

        def S(f):
            blk.sync(f)

        def T(f):
            blk.tensor(f)

        def V(f):
            blk.vector(f)

        def A(f):
            blk.scalar(f)

        def _(g):
            P.inc(g.dma_start(out=idb[:, :], in_=iden[:, :]), "lg", 16)
            for i in range(NTT):
                P.inc(g.dma_start(out=ocb[:, i * 512:(i + 1) * 512],
                                  in_=ocsl[i * 128:(i + 1) * 128, :]), "lg", 16)
            for k in range(4):
                P.inc(g.dma_start(out=owb[:, k * D:(k + 1) * D],
                                  in_=outw[k * 128:(k + 1) * 128, :]), "lg", 16)
            for k in range(8):
                P.inc(g.dma_start(out=wqb[:, k * 256:(k + 1) * 256],
                                  in_=wq[k * 128:(k + 1) * 128, :]), "lg", 16)
                P.inc(g.dma_start(out=wkb[:, k * 256:(k + 1) * 256],
                                  in_=wk[k * 128:(k + 1) * 128, :]), "lg", 16)
        G(_)

        def _(s):
            P.inc(s.dma_start(out=obv[:, :], in_=obvi[:, :]), "ld", 16)
            for k in range(8):
                P.inc(s.dma_start(out=lnf[:, k * TOK:(k + 1) * TOK],
                                  in_=ln0sl[:, k * 128:(k + 1) * 128],
                                  transpose=True), "tl", 16)
        S(_)

        lg_total, ld_total, tl_total = P.n("lg"), P.n("ld"), P.n("tl")
        for j in range(4 * NTT):
            i, k = divmod(j, 4)
            slot = (ptA if j % 2 == 0 else ptB)

            def _(pe, i=i, k=k, j=j, slot=slot):
                if j == 0:
                    P.wait(pe, "lg", lg_total)
                if j >= 2:
                    P.wait(pe, "dv", P.n("dv") - 1)
                P.inc(pe.transpose(slot[:, :],
                                   ocb[:, i * 512 + k * 128: i * 512 + (k + 1) * 128],
                                   idb[:, :]), "pe")
            T(_)

            def _(v, i=i, k=k, slot=slot):
                P.wait(v, "pe")
                P.inc(v.tensor_copy(ocf[:, k * TOK + i * 128: k * TOK + (i + 1) * 128],
                                    slot[:, :]), "dv")
            V(_)

        for ch in range(8):
            for fg in range(2):
                bank = pa if fg == 0 else pb

                def _(pe, ch=ch, fg=fg, bank=bank):
                    if ch == 0 and fg == 0:
                        P.wait(pe, "dv")
                    if ch > 0 or fg > 0:
                        P.wait(pe, "ac")
                    for kc in range(4):
                        mm = pe.matmul(bank[:, :],
                                       owb[0:128, kc * D + ch * 128: kc * D + (ch + 1) * 128],
                                       ocf[:, kc * TOK + fg * 512: kc * TOK + (fg + 1) * 512],
                                       start=(kc == 0), stop=(kc == 3))
                    P.inc(mm, "pe")
                T(_)

                def _(a, ch=ch, fg=fg, bank=bank):
                    P.wait(a, "pe")
                    if ch == 0 and fg == 0:
                        P.wait(a, "ld", ld_total)
                    if ch > 0 and fg == 0:
                        P.wait(a, "dv")
                    P.inc(a.activation(xgf[:, fg * 512:(fg + 1) * 512], bank[:, :],
                                       AF.Identity, bias=obv[:, ch: ch + 1],
                                       scale=1.0), "ac")
                A(_)

            def _(v, ch=ch):
                P.wait(v, "ac")
                if ch == 0:
                    P.wait(v, "tl", tl_total)
                P.inc(v.tensor_tensor(encf[:, ch * TOK:(ch + 1) * TOK],
                                      xgf[:, :],
                                      lnf[:, ch * TOK:(ch + 1) * TOK], ALU.add), "dv")
                P.inc(v.tensor_copy(xgb[:, ch * TOK:(ch + 1) * TOK], xgf[:, :]), "dv")
            V(_)

        for i in range(NTT):
            p = i % 2
            stsem = "stA" if p == 0 else "stB"
            for ch in range(8):
                slot = (ptA if ch % 2 == 0 else ptB)

                def _(pe, i=i, ch=ch, slot=slot):
                    if i == 0 and ch == 0:
                        P.wait(pe, "dv")
                    if i + ch > 0:
                        P.wait(pe, "dv", P.n("dv") - 1)
                    P.inc(pe.transpose(slot[:, :],
                                       xgb[:, ch * TOK + i * 128: ch * TOK + (i + 1) * 128],
                                       idb[:, :]), "pe")
                T(_)

                def _(v, i=i, ch=ch, p=p, slot=slot, stsem=stsem):
                    P.wait(v, "pe")
                    if i >= 2 and ch == 0:
                        P.wait(v, stsem, P.n(stsem))
                    P.inc(v.tensor_copy(stg[:, p * D + ch * 128: p * D + (ch + 1) * 128],
                                        slot[:, :]), "dv")
                V(_)

            def _(g, i=i, p=p, stsem=stsem):
                P.wait(g, "dv")
                P.inc(g.dma_start(out=xg0tm[i * 128:(i + 1) * 128, :],
                                  in_=stg[:, p * D:(p + 1) * D]), stsem, 16)
            G(_)

        for i in range(NTT):
            for which in range(2):
                bank = pa if which == 0 else pb
                wsb = wqb if which == 0 else wkb
                dst = qesl if which == 0 else kesl
                qslot = (i % 2) * 2 + which
                stsem = "stA" if i % 2 == 0 else "stB"

                def _(pe, i=i, bank=bank, wsb=wsb):
                    P.wait(pe, "dv")
                    if i > 0:
                        pass
                    for kc in range(8):
                        mm = pe.matmul(bank[:, 0:256],
                                       encf[:, kc * TOK + i * 128: kc * TOK + (i + 1) * 128],
                                       wsb[0:128, kc * 256:(kc + 1) * 256],
                                       start=(kc == 0), stop=(kc == 7))
                    P.inc(mm, "pe")
                T(_)

                def _(v, bank=bank, qslot=qslot, stsem=stsem, i=i):
                    P.wait(v, "pe")
                    if i >= 2:
                        P.wait(v, stsem, P.n(stsem))
                    P.inc(v.tensor_copy(qst[:, qslot * 256:(qslot + 1) * 256],
                                        bank[:, 0:256]), "dv")
                V(_)

                def _(g, i=i, dst=dst, qslot=qslot, stsem=stsem):
                    P.wait(g, "dv")
                    P.inc(g.dma_start(out=dst[i * 128:(i + 1) * 128, :],
                                      in_=qst[:, qslot * 256:(qslot + 1) * 256]),
                          stsem, 16)
                G(_)

        def _(g):
            P.wait(g, "stA")
            P.wait(g, "stB")
        G(_)

    return nc


def host_rC(inputs, rA_outs, rB_outs, cid):
    out_w = inputs["L0_attn_out_w"]
    out_b = inputs["L0_attn_out_b"]
    wq = inputs["L0_attn_wq"]
    wk = inputs["L0_attn_wk"]
    ocsl = np.concatenate([np.asarray(rB_outs[h]["outc"])[cid * TOK:(cid + 1) * TOK, :]
                           for h in range(HEADS)], axis=1)
    obvi = np.ascontiguousarray(np.asarray(out_b).reshape(8, 128).T)
    return {
        "ocsl": np.ascontiguousarray(ocsl),
        "outw": np.asarray(out_w),
        "obvi": obvi,
        "wq": np.asarray(wq),
        "wk": np.asarray(wk),
        "ln0sl": np.asarray(rA_outs[cid]["ln1sp"])[0],
        "iden": np.eye(128, dtype=np.float32),
    }




def build_rD():
    nc = _bass()
    qe = nc.dram_tensor("qe", [N, 256], F32, kind="ExternalInput")
    ke = nc.dram_tensor("ke", [N, 256], F32, kind="ExternalInput")
    adjr = nc.dram_tensor("adjr", [128, 128], I32, kind="ExternalInput")
    adjc = nc.dram_tensor("adjc", [128, 128], I32, kind="ExternalInput")
    adjv = nc.dram_tensor("adjv", [128, 128], F32, kind="ExternalInput")
    io16 = nc.dram_tensor("io16", [128, 16], F32, kind="ExternalInput")
    io512 = nc.dram_tensor("io512", [128, 512], F32, kind="ExternalInput")

    arawp = nc.dram_tensor("arawp", [16, 512], F32, kind="ExternalOutput")

    NE = 128
    with ExitStack() as ctx:
        def sb(name, shape, dt=F32):
            return ctx.enter_context(nc.sbuf_tensor(name, shape, dt))

        def psum(name, shape, dt=F32):
            return ctx.enter_context(nc.psum_tensor(name, shape, dt))

        ri = sb("ri", [128, 128], I32)
        ci = sb("ci", [128, 128], I32)
        vv = sb("vv", [128, 128], F32)
        iob = sb("iob", [128, 16], F32)
        iof = sb("iof", [128, 512], F32)
        rf = sb("rf", [128, 128], F32)
        rdiv = sb("rdiv", [128, 128], F32)
        rmod = sb("rmod", [128, 128], F32)
        gq = sb("gq", [128, 2 * 256], F32)
        gk = sb("gk", [128, 2 * 256], F32)
        dscr = sb("dscr", [128, 256], F32)
        smalls = sb("smalls", [128, 4], F32)
        lhq = sb("lhq", [128, 16], F32)
        lhb = sb("lhb", [128, 2 * 16], BF16)
        rhb = sb("rhb", [128, 2 * 512], BF16)
        stg = sb("stgd", [16, 512], F32)
        paraw = psum("paraw", [16, 512], F32)
        blk = ctx.enter_context(nc.Block())
        P = Prog(nc, blk, ctx)
        P.sem("ld", "gA", "gB", "pe", "dv", "st", "chv")

        def _(s):
            for (dst, srce) in ((ri, adjr), (ci, adjc), (vv, adjv),
                                (iob, io16), (iof, io512)):
                P.inc(s.dma_start(out=dst[:, :], in_=srce[:, :]), "ld", 16)
        blk.sync(_)

        def _(v):
            P.wait(v, "ld")
            ch = Chain(v, P.sems["chv"][0])
            ch(v.tensor_copy(rf[:, :], ri[:, :]))
            ch(v.tensor_scalar(rdiv[:, :], rf[:, :], 1.0 / 512.0,
                               -(0.5 - 2.0 ** -10), ALU.mult, ALU.add))
            ch(v.tensor_scalar(rdiv[:, :], rdiv[:, :], RND_C, RND_C,
                               ALU.add, ALU.subtract))
            ch(v.scalar_tensor_tensor(rmod[:, :], rdiv[:, :], -512.0, rf[:, :],
                                      ALU.mult, ALU.add))
            P.inc(v.engine_nop(), "dv")
        blk.vector(_)
        chv_handle = P.sems["chv"][0]
        chv_n = [4]

        for t in range(NE):
            p = t % 2
            gsem = "gA" if p == 0 else "gB"

            def _(g, t=t, p=p, gsem=gsem):
                if t == 0:
                    P.wait(g, "ld")
                if t >= 2:
                    P.wait(g, "pe", P.n("pe") - 1)
                from concourse.bass import IndirectOffsetOnAxis
                P.inc(g.indirect_dma_start(
                    out=gq[:, p * 256:(p + 1) * 256], out_offset=None,
                    in_=qe[:, :],
                    in_offset=IndirectOffsetOnAxis(ap=ri[:, t: t + 1], axis=0)),
                    gsem, 16)
                P.inc(g.indirect_dma_start(
                    out=gk[:, p * 256:(p + 1) * 256], out_offset=None,
                    in_=ke[:, :],
                    in_offset=IndirectOffsetOnAxis(ap=ci[:, t: t + 1], axis=0)),
                    gsem, 16)
            blk.gpsimd(_)

            def _(v, t=t, p=p, gsem=gsem):
                ch = Chain(v, chv_handle)
                ch.n = chv_n[0]
                P.wait(v, gsem)
                if t >= 2:
                    P.wait(v, "pe", P.n("pe") - 1)
                ch(v.tensor_tensor(dscr[:, :], gq[:, p * 256:(p + 1) * 256],
                                   gk[:, p * 256:(p + 1) * 256], ALU.mult))
                ch(v.tensor_reduce(smalls[:, 0:1], dscr[:, :], AX.X, ALU.add))
                ch(v.tensor_scalar(smalls[:, 2:3], smalls[:, 0:1], vv[:, t: t + 1],
                                   0.0625, ALU.mult, ALU.mult))
                ch(v.tensor_scalar(lhq[:, :], iob[:, :], rdiv[:, t: t + 1], None,
                                   ALU.is_equal))
                v.tensor_scalar(lhb[:, p * 16:(p + 1) * 16], lhq[:, :],
                                smalls[:, 2:3], None, ALU.mult).then_inc(
                                    P.sems["dv"][0], 1)
                P.sems["dv"][1] += 1
                v.tensor_scalar(rhb[:, p * 512:(p + 1) * 512], iof[:, :],
                                rmod[:, t: t + 1], None,
                                ALU.is_equal).then_inc(P.sems["dv"][0], 1)
                P.sems["dv"][1] += 1
                chv_n[0] = ch.n
            blk.vector(_)

            def _(pe, t=t, p=p):
                P.wait(pe, "dv")
                P.inc(pe.matmul(paraw[:, :], lhb[:, p * 16:(p + 1) * 16],
                                rhb[:, p * 512:(p + 1) * 512],
                                start=(t == 0), stop=(t == NE - 1)), "pe")
            blk.tensor(_)

        def _(v):
            P.wait(v, "pe")
            P.inc(v.tensor_copy(stg[:, :], paraw[:, :]), "dv")
        blk.vector(_)

        def _(g):
            P.wait(g, "dv")
            P.inc(g.dma_start(out=arawp[:, :], in_=stg[:, :]), "st", 16)
            P.wait(g, "st")
        blk.gpsimd(_)

    return nc


def host_rD(inputs, qe, ke, cid):
    rows = np.asarray(inputs["L0_attn_adj_rows"][cid * 16384:(cid + 1) * 16384])
    cols = np.asarray(inputs["L0_attn_adj_cols"][cid * 16384:(cid + 1) * 16384])
    vals = np.asarray(inputs["L0_attn_adj_vals"][cid * 16384:(cid + 1) * 16384])
    io16 = np.broadcast_to(np.arange(16, dtype=np.float32)[None, :], (128, 16))
    io512 = np.broadcast_to(np.arange(512, dtype=np.float32)[None, :], (128, 512))
    return {
        "qe": qe, "ke": ke,
        "adjr": np.ascontiguousarray(rows.reshape(128, 128).T),
        "adjc": np.ascontiguousarray(cols.reshape(128, 128).T),
        "adjv": np.ascontiguousarray(vals.reshape(128, 128).T),
        "io16": np.ascontiguousarray(io16),
        "io512": np.ascontiguousarray(io512),
    }




def build_rE():
    nc = _bass()
    xsh = nc.dram_tensor("xsh", [B, TOK, D], F32, kind="ExternalInput")
    ln1sp = nc.dram_tensor("ln1sp", [B, TOK, D], BF16, kind="ExternalInput")
    xg0 = nc.dram_tensor("xg0", [TOK, D], BF16, kind="ExternalInput")
    araw8 = nc.dram_tensor("araw8", [8, 16, 512], F32, kind="ExternalInput")
    wvw = nc.dram_tensor("wvw", [D, D], F32, kind="ExternalInput")
    wvb = nc.dram_tensor("wvb", [1, D], F32, kind="ExternalInput")
    sel2 = nc.dram_tensor("sel2", [16, 2], F32, kind="ExternalInput")
    iden = nc.dram_tensor("iden", [128, 128], F32, kind="ExternalInput")

    x1sp = nc.dram_tensor("x1sp", [B, TOK, D], F32, kind="ExternalOutput")
    gmax1p = nc.dram_tensor("gmax1p", [1, 1], F32, kind="ExternalOutput")

    NJ = TOK // 128
    with ExitStack() as ctx:
        def sb(name, shape, dt=F32):
            return ctx.enter_context(nc.sbuf_tensor(name, shape, dt))

        def psum(name, shape, dt=F32):
            return ctx.enter_context(nc.psum_tensor(name, shape, dt))

        wvwb = sb("wvwb", [128, 8 * D], BF16)
        lnfm = sb("lnfm", [128, 2 * 8 * 128 * 8], BF16)
        xgt = sb("xgt", [128, 8 * D], BF16)
        vb1 = sb("vb1", [1, D], F32)
        vbias = sb("vbias", [128, D], F32)
        ar8 = sb("ar8", [16, 8 * 512], F32)
        arsum = sb("arsum", [16, 512], F32)
        earw = sb("earw", [16, 512], F32)
        alpha = sb("alpha", [16, 512], F32)
        alm = sb("alm", [1, TOK], F32)
        selb = sb("selb", [16, 2], F32)
        alpc = sb("alpc", [128, 8], F32)
        idf = sb("idf", [128, 128], F32)
        one16 = sb("one16", [1, 16], F32)
        one128 = sb("one128", [1, 128], F32)
        one11 = sb("one11", [1, 2], F32)
        sm = sb("sme", [16, 4], F32)
        sc = sb("sce", [1, 8], F32)
        xb2 = sb("xb2", [128, 2 * D], F32)
        lnt = sb("lnt", [128, 2 * D], BF16)
        x1t = sb("x1t", [128, 2 * D], F32)
        tva = sb("tva", [128, D], F32)
        xlv = sb("xlv", [128, D], F32)
        wei = sb("wei", [128, D], F32)
        swv = sb("swv", [128, D], F32)
        encv = sb("encv", [128, D], F32)
        r1v = sb("r1v", [128, D], F32)
        zscr = sb("zscr", [128, D], F32)
        stat = sb("state", [128, 8], F32)
        stdv = sb("stdvE", [128, 2], F32)
        rstdv = sb("rstdvE", [128, 2], F32)
        gacc = sb("gacc", [128, 2], F32)
        gfin = sb("gfin", [1, 2], F32)
        epsv = sb("epsvE", [128, 2], F32)
        pv0 = psum("pv0", [128, 512])
        pv1 = psum("pv1", [128, 512])
        pmisc = psum("pmisc", [128, 128])
        blk = ctx.enter_context(nc.Block())
        P = Prog(nc, blk, ctx)
        P.sem("lg", "ld", "lfA", "lfB", "lxA", "lxB", "llA", "llB",
              "pe", "dv", "ac", "pl", "stA", "stB", "chv", "cha", "chp", "al")
        chv = [0]
        chp = [0]

        def CH(eng, inst, which="chv", cnt=chv):
            cnt[0] += 1
            inst.then_inc(P.sems[which][0], 1)
            eng.wait_ge(P.sems[which][0], cnt[0])
            return inst

        def _(g):
            for k in range(8):
                P.inc(g.dma_start(out=wvwb[:, k * D:(k + 1) * D],
                                  in_=wvw[k * 128:(k + 1) * 128, :]), "lg", 16)
        blk.gpsimd(_)

        def _(s):
            P.inc(s.dma_start(out=vb1[:, :], in_=wvb[:, :]), "ld", 16)
            P.inc(s.dma_start(out=idf[:, :], in_=iden[:, :]), "ld", 16)
            P.inc(s.dma_start(out=selb[:, :], in_=sel2[:, :]), "ld", 16)
            for r in range(8):
                P.inc(s.dma_start(out=ar8[:, r * 512:(r + 1) * 512],
                                  in_=araw8[r, :, :]), "ld", 16)
            for j in range(NJ):
                P.inc(s.dma_start(out=xgt[:, j * D:(j + 1) * D],
                                  in_=xg0[j * 128:(j + 1) * 128, :]), "ld", 16)
        blk.sync(_)

        ld_total = P.n("ld")
        lg_total = P.n("lg")

        def _(v):
            P.wait(v, "ld", ld_total)
            CH(v, v.memset(one16[:, :], 1.0))
            CH(v, v.memset(one128[:, :], 1.0))
            CH(v, v.memset(one11[:, :], 1.0))
            CH(v, v.memset(epsv[:, :], EPS))
            CH(v, v.memset(gacc[:, :], 0.0))
            acc = arsum
            CH(v, v.tensor_tensor(acc[:, :], ar8[:, 0:512], ar8[:, 512:1024], ALU.add))
            for r in range(2, 8):
                CH(v, v.tensor_tensor(acc[:, :], acc[:, :],
                                      ar8[:, r * 512:(r + 1) * 512], ALU.add))
            CH(v, v.tensor_reduce(sm[:, 0:1], acc[:, :], AX.X, ALU.max))
            P.inc(v.engine_nop(), "dv")
        blk.vector(_)

        def _(pe):
            P.wait(pe, "dv")
            P.inc(pe.transpose(pmisc[0:1, 0:16], sm[0:16, 0:1], idf[0:16, 0:16]), "pe")
        blk.tensor(_)

        def _(v):
            P.wait(v, "pe")
            CH(v, v.tensor_reduce(sc[0:1, 0:1], pmisc[0:1, 0:16], AX.X, ALU.max))
            CH(v, v.tensor_scalar(sc[0:1, 2:3], sc[0:1, 0:1], -1.0, None, ALU.mult))
            P.inc(v.engine_nop(), "dv")
        blk.vector(_)

        def _(pe):
            P.wait(pe, "dv")
            P.inc(pe.matmul(pmisc[0:16, 0:1], one16[0:1, 0:16], sc[0:1, 2:3],
                            start=True, stop=True), "pe")
        blk.tensor(_)

        def _(v):
            P.wait(v, "pe")
            CH(v, v.tensor_copy(sm[:, 1:2], pmisc[0:16, 0:1]))
            P.inc(v.engine_nop(), "dv")
        blk.vector(_)

        def _(a):
            P.wait(a, "dv")
            P.inc(a.activation(earw[:, :], arsum[:, :], AF.Exp,
                               bias=sm[0:16, 1:2]), "ac")
        blk.scalar(_)

        def _(v):
            P.wait(v, "ac")
            CH(v, v.tensor_reduce(sm[:, 2:3], earw[:, :], AX.X, ALU.add))
            P.inc(v.engine_nop(), "dv")
        blk.vector(_)

        def _(pe):
            P.wait(pe, "dv")
            P.inc(pe.transpose(pmisc[0:1, 16:32], sm[0:16, 2:3], idf[0:16, 0:16]), "pe")
        blk.tensor(_)

        def _(v):
            P.wait(v, "pe")
            CH(v, v.tensor_reduce(sc[0:1, 4:5], pmisc[0:1, 16:32], AX.X, ALU.add))
            CH(v, v.reciprocal(sc[0:1, 6:7], sc[0:1, 4:5]))
            P.inc(v.engine_nop(), "dv")
        blk.vector(_)

        def _(pe):
            P.wait(pe, "dv")
            P.inc(pe.matmul(pmisc[0:16, 1:2], one16[0:1, 0:16], sc[0:1, 6:7],
                            start=True, stop=True), "pe")
        blk.tensor(_)

        def _(v):
            P.wait(v, "pe")
            CH(v, v.tensor_copy(sm[:, 3:4], pmisc[0:16, 1:2]))
            P.inc(v.tensor_scalar(alpha[:, :], earw[:, :], sm[0:16, 3:4], None,
                                  ALU.mult), "dv")
        blk.vector(_)

        for gg in range(2):
            def _(pe, gg=gg):
                P.wait(pe, "dv")
                if gg == 1:
                    P.wait(pe, "al", 1)
                P.inc(pe.matmul(pv1[0:1, 0:512],
                                selb[0:16, gg: gg + 1], alpha[0:16, :],
                                start=True, stop=True), "pe")
            blk.tensor(_)

            def _(v, gg=gg):
                P.wait(v, "pe")
                P.inc(v.tensor_copy(alm[0:1, gg * 512:(gg + 1) * 512],
                                    pv1[0:1, 0:512]), "al", 1)
            blk.vector(_)

        for j in range(NJ):
            def _(pe, j=j):
                P.wait(pe, "al", 2)
                if j > 0:
                    P.wait(pe, "dv")
                P.inc(pe.matmul(pmisc[0:128, 2 + (j % 2): 3 + (j % 2)],
                                alm[0:1, j * 128:(j + 1) * 128], one11[0:1, 0:1],
                                start=True, stop=True), "pe")
            blk.tensor(_)

            def _(v, j=j):
                P.wait(v, "pe")
                P.inc(v.tensor_copy(alpc[:, j: j + 1],
                                    pmisc[0:128, 2 + (j % 2): 3 + (j % 2)]), "dv")
            blk.vector(_)

        for fg in range(2):
            def _(pe, fg=fg):
                if fg == 0:
                    P.wait(pe, "dv")
                else:
                    P.wait(pe, "dv")
                P.inc(pe.matmul(pv0[:, :], one128[0:1, 0:128],
                                vb1[0:1, fg * 512:(fg + 1) * 512],
                                start=True, stop=True), "pe")
            blk.tensor(_)

            def _(v, fg=fg):
                P.wait(v, "pe")
                P.inc(v.tensor_copy(vbias[:, fg * 512:(fg + 1) * 512], pv0[:, :]), "dv")
            blk.vector(_)

        for b in range(B):
            sl_f = b % 2
            def _(s, b=b, sl_f=sl_f):
                if b >= 2:
                    P.wait(s, "pe")
                for k in range(8):
                    P.inc(s.dma_start(
                        out=lnfm[:, (sl_f * 8 + k) * TOK:(sl_f * 8 + k + 1) * TOK],
                        in_=ln1sp[b, :, k * 128:(k + 1) * 128],
                        transpose=True), "lfA" if sl_f == 0 else "lfB", 16)
            blk.sync(_)

            for j in range(NJ):
                m = b * NJ + j
                sl = m % 2
                stsem = "stA" if sl == 0 else "stB"

                def _(s, b=b, j=j, sl=sl, m=m):
                    if m >= 2:
                        P.wait(s, "dv")
                    P.inc(s.dma_start(out=xb2[:, sl * D:(sl + 1) * D],
                                      in_=xsh[b, j * 128:(j + 1) * 128, :]),
                          "lxA" if sl == 0 else "lxB", 16)
                    P.inc(s.dma_start(out=lnt[:, sl * D:(sl + 1) * D],
                                      in_=ln1sp[b, j * 128:(j + 1) * 128, :]),
                          "llA" if sl == 0 else "llB", 16)
                blk.sync(_)

                for fg in range(2):
                    bank = pv0 if fg == 0 else pv1

                    def _(pe, b=b, j=j, fg=fg, bank=bank, sl_f=sl_f, m=m):
                        if m == 0 and fg == 0:
                            P.wait(pe, "lg", lg_total)
                        P.wait(pe, "lfA" if sl_f == 0 else "lfB",
                               128 * (b // 2 + 1))
                        P.wait(pe, "dv")
                        for kc in range(8):
                            mm = pe.matmul(
                                bank[:, :],
                                lnfm[:, (sl_f * 8 + kc) * TOK + j * 128:
                                     (sl_f * 8 + kc) * TOK + (j + 1) * 128],
                                wvwb[0:128, kc * D + fg * 512: kc * D + (fg + 1) * 512],
                                start=(kc == 0), stop=(kc == 7))
                        P.inc(mm, "pe")
                    blk.tensor(_)

                    def _(v, fg=fg, bank=bank):
                        P.wait(v, "pe")
                        P.inc(v.tensor_tensor(tva[:, fg * 512:(fg + 1) * 512],
                                              bank[:, :],
                                              vbias[:, fg * 512:(fg + 1) * 512],
                                              ALU.add), "dv")
                    blk.vector(_)

                def _(v, j=j, sl=sl):
                    P.wait(v, "dv")
                    CH(v, v.tensor_scalar(xlv[:, :], tva[:, :], alpc[:, j: j + 1],
                                          None, ALU.mult))
                    P.inc(v.engine_nop(), "dv")
                blk.vector(_)

                def _(a):
                    P.wait(a, "dv")
                    P.inc(a.activation(wei[:, :], xlv[:, :], AF.Sigmoid,
                                       scale=-1.0), "ac")
                blk.scalar(_)

                def _(g, b=b, j=j, sl=sl):
                    P.wait(g, "ac")
                    P.wait(g, "llA" if sl == 0 else "llB")
                    cp = Chain(g, P.sems["chp"][0])
                    cp.n = chp[0]
                    cp(g.tensor_tensor(swv[:, :], wei[:, :], wei[:, :], ALU.mult))
                    cp(g.tensor_tensor(encv[:, :], xgt[:, j * D:(j + 1) * D],
                                       lnt[:, sl * D:(sl + 1) * D], ALU.add))
                    cp(g.tensor_tensor(r1v[:, :], xlv[:, :], encv[:, :], ALU.subtract))
                    cp(g.tensor_tensor(r1v[:, :], swv[:, :], r1v[:, :], ALU.mult))
                    g.tensor_tensor(r1v[:, :], r1v[:, :], encv[:, :],
                                    ALU.add).then_inc(P.sems["pl"][0], 1)
                    P.sems["pl"][1] += 1
                    chp[0] = cp.n
                blk.gpsimd(_)

                def _(v, b=b, j=j, sl=sl, m=m, stsem=stsem):
                    P.wait(v, "pl")
                    P.wait(v, "lxA" if sl == 0 else "lxB")
                    if m >= 2:
                        P.wait(v, stsem, P.n(stsem))
                    CH(v, v.scalar_tensor_tensor(x1t[:, sl * D:(sl + 1) * D],
                                                 r1v[:, :], 2.0,
                                                 xb2[:, sl * D:(sl + 1) * D],
                                                 ALU.mult, ALU.add))
                    P.inc(v.engine_nop(), "dv")
                blk.vector(_)

                def _(a, sl=sl):
                    P.wait(a, "dv")
                    P.inc(a.activation(zscr[:, :], x1t[:, sl * D:(sl + 1) * D],
                                       AF.Square, accum_out=stat[:, 0:1]), "ac")
                    P.wait(a, "ac")
                    P.inc(a.activation(zscr[:, :], x1t[:, sl * D:(sl + 1) * D],
                                       AF.Identity, scale=1.0,
                                       accum_out=stat[:, 1:2]), "ac")
                blk.scalar(_)

                def _(v, sl=sl):
                    P.wait(v, "ac")
                    CH(v, v.tensor_scalar(stat[:, 2:3], stat[:, 1:2], 1.0 / D,
                                          None, ALU.mult))
                    CH(v, v.tensor_scalar(stat[:, 4:5], stat[:, 0:1], 1.0 / D,
                                          None, ALU.mult))
                    CH(v, v.tensor_tensor(stat[:, 5:6], stat[:, 2:3], stat[:, 2:3],
                                          ALU.mult))
                    CH(v, v.tensor_tensor(stat[:, 3:4], stat[:, 4:5], stat[:, 5:6],
                                          ALU.subtract))
                    P.inc(v.engine_nop(), "dv")
                blk.vector(_)

                def _(a):
                    P.wait(a, "dv")
                    P.inc(a.activation(stdv[:, 0:1], stat[:, 3:4], AF.Sqrt,
                                       bias=epsv[:, 0:1]), "ac")
                blk.scalar(_)

                def _(v, sl=sl):
                    P.wait(v, "ac")
                    CH(v, v.reciprocal(rstdv[:, 0:1], stdv[:, 0:1]))
                    CH(v, v.tensor_scalar(stat[:, 6:7], stat[:, 2:3], rstdv[:, 0:1],
                                          -1.0, ALU.mult, ALU.mult))
                    CH(v, v.tensor_scalar(zscr[:, :], x1t[:, sl * D:(sl + 1) * D],
                                          rstdv[:, 0:1], stat[:, 6:7],
                                          ALU.mult, ALU.add))
                    CH(v, v.tensor_reduce(stat[:, 7:8], zscr[:, :], AX.X, ALU.max,
                                          apply_absolute_value=True))
                    CH(v, v.tensor_tensor(gacc[:, 0:1], gacc[:, 0:1], stat[:, 7:8],
                                          ALU.max))
                    P.inc(v.engine_nop(), "dv")
                blk.vector(_)

                def _(g, b=b, j=j, sl=sl, stsem=stsem):
                    P.wait(g, "dv")
                    P.inc(g.dma_start(out=x1sp[b, j * 128:(j + 1) * 128, :],
                                      in_=x1t[:, sl * D:(sl + 1) * D]), stsem, 16)
                blk.gpsimd(_)

        def _(pe):
            P.wait(pe, "dv")
            P.inc(pe.transpose(pmisc[0:1, 0:128], gacc[:, 0:1], idf[:, :]), "pe")
        blk.tensor(_)

        def _(v):
            P.wait(v, "pe")
            CH(v, v.tensor_reduce(gfin[0:1, 0:1], pmisc[0:1, 0:128], AX.X, ALU.max))
            P.inc(v.engine_nop(), "dv")
        blk.vector(_)

        def _(g):
            P.wait(g, "dv")
            P.inc(g.dma_start(out=gmax1p[:, :], in_=gfin[0:1, 0:1]), "stA", 16)
            P.wait(g, "stA")
            P.wait(g, "stB")
        blk.gpsimd(_)

    return nc


def host_rE(inputs, rA_outs, rC_outs, rD_outs, cid):
    x = inputs["L0_attn_dense"]
    return {
        "xsh": np.ascontiguousarray(x[:, cid * TOK:(cid + 1) * TOK, :]),
        "ln1sp": np.asarray(rA_outs[cid]["ln1sp"]),
        "xg0": np.asarray(rC_outs[cid]["xg0tm"]),
        "araw8": np.stack([np.asarray(rD_outs[c]["arawp"]) for c in range(NCORES)]),
        "wvw": np.asarray(inputs["L0_attn_wv_w"]),
        "wvb": np.asarray(inputs["L0_attn_wv_b"]).reshape(1, D),
        "sel2": np.eye(16, dtype=np.float32)[:, [2 * cid, 2 * cid + 1]].copy(),
        "iden": np.eye(128, dtype=np.float32),
    }




def build_rF():
    nc = _bass()
    x1sp = nc.dram_tensor("x1sp", [B, TOK, D], F32, kind="ExternalInput")
    gmax18 = nc.dram_tensor("gmax18", [1, 8], F32, kind="ExternalInput")
    clipx = nc.dram_tensor("clipx", [1, 1], F32, kind="ExternalInput")
    wgq = nc.dram_tensor("wgq", [IFI, D], BF16, kind="ExternalInput")
    wuq = nc.dram_tensor("wuq", [IFI, D], BF16, kind="ExternalInput")
    wgs = nc.dram_tensor("wgs", [IFI, 1], F32, kind="ExternalInput")
    wus = nc.dram_tensor("wus", [IFI, 1], F32, kind="ExternalInput")
    iden = nc.dram_tensor("idenF", [128, 128], F32, kind="ExternalInput")

    sgusp = nc.dram_tensor("sgusp", [22, 128, B * TOK], F32, kind="ExternalOutput")
    gmax2p = nc.dram_tensor("gmax2p", [1, 1], F32, kind="ExternalOutput")

    xqscr = nc.dram_tensor("xqscr", [B * TOK, D], BF16, kind="Internal")

    NM = B * TOK // 128
    NG = 4
    NIC = IFI // 128
    with ExitStack() as ctx:
        def sb(name, shape, dt=F32):
            return ctx.enter_context(nc.sbuf_tensor(name, shape, dt))

        def psum(name, shape, dt=F32):
            return ctx.enter_context(nc.psum_tensor(name, shape, dt))

        wgf = sb("wgf", [128, 8 * IFI], BF16)
        wuf = sb("wuf", [128, 8 * IFI], BF16)
        xqf = sb("xqf", [128, 2 * 8 * 2048], BF16)
        x1b = sb("x1b", [128, 2 * D], F32)
        xqt = sb("xqt", [128, 2 * D], BF16)
        uscr = sb("uscr", [128, D], F32)
        sscr = sb("sscr", [128, D], F32)
        g18 = sb("g18", [1, 8], F32)
        clx = sb("clx", [1, 2], F32)
        scp = sb("scp", [1, 8], F32)
        iv1 = sb("iv1", [128, 2], F32)
        sxg = sb("sxg", [128, NIC], F32)
        sxu = sb("sxu", [128, NIC], F32)
        wgss = sb("wgss", [128, NIC], F32)
        wuss = sb("wuss", [128, NIC], F32)
        one128 = sb("one128f", [1, 128], F32)
        stat = sb("statf", [128, 8], F32)
        stdv = sb("stdvF", [128, 2], F32)
        rstdv = sb("rstdvF", [128, 2], F32)
        rstd2 = sb("rstd2F", [128, 2], F32)
        epsv = sb("epsvF", [128, 2], F32)
        gsil = sb("gsil", [128, 2 * 512], F32)
        uf32 = sb("uf32", [128, 2 * 512], F32)
        sguf = sb("sguf", [128, 2 * 512], F32)
        gacc = sb("gaccF", [128, 2], F32)
        gfin = sb("gfinF", [1, 2], F32)
        idf = sb("idfF", [128, 128], F32)
        pa0 = psum("pa0F", [128, 512])
        pa1 = psum("pa1F", [128, 512])
        pb0 = psum("pb0F", [128, 512])
        pb1 = psum("pb1F", [128, 512])
        pm = psum("pmF", [128, 128])
        blk = ctx.enter_context(nc.Block())
        P = Prog(nc, blk, ctx)
        P.sem("lg", "ld", "lxA", "lxB", "tqA", "tqB", "pe", "dv", "ac",
              "sqA", "sqB", "stA", "stB", "stG", "chv")

        def _(s):
            P.inc(s.dma_start(out=g18[:, :], in_=gmax18[:, :]), "ld", 16)
            P.inc(s.dma_start(out=clx[:, 0:1], in_=clipx[:, :]), "ld", 16)
            P.inc(s.dma_start(out=idf[:, :], in_=iden[:, :]), "ld", 16)
            P.inc(s.dma_start(out=wgss[:, :],
                              in_=wgs.rearrange("(c p) o -> p (c o)", p=128),
                              allow_slow_non_contiguous=True), "ld", 16)
            P.inc(s.dma_start(out=wuss[:, :],
                              in_=wus.rearrange("(c p) o -> p (c o)", p=128),
                              allow_slow_non_contiguous=True), "ld", 16)
        blk.sync(_)

        def _(a):
            for k in range(8):
                P.inc(a.dma_start(out=wgf[:, k * IFI:(k + 1) * IFI],
                                  in_=wgq[:, k * 128:(k + 1) * 128],
                                  transpose=True), "lg", 16)
                P.inc(a.dma_start(out=wuf[:, k * IFI:(k + 1) * IFI],
                                  in_=wuq[:, k * 128:(k + 1) * 128],
                                  transpose=True), "lg", 16)
        blk.scalar(_)

        ld_total = P.n("ld")

        def _(a):
            P.wait(a, "ld", ld_total)
            P.inc(a.activation(clx[0:1, 1:2], clx[0:1, 0:1], AF.Sigmoid), "ac")
        blk.scalar(_)

        def _(v):
            ch = Chain(v, P.sems["chv"][0])
            ch.n = P.n("chv")
            P.wait(v, "ac")
            ch(v.memset(epsv[:, :], EPS))
            ch(v.memset(one128[:, :], 1.0))
            ch(v.memset(gacc[:, :], 0.0))
            ch(v.tensor_reduce(scp[0:1, 0:1], g18[0:1, :], AX.X, ALU.max))
            ch(v.tensor_tensor(scp[0:1, 2:3], scp[0:1, 0:1], clx[0:1, 1:2], ALU.mult))
            ch(v.tensor_scalar(scp[0:1, 4:5], scp[0:1, 2:3], 1.0 / QMAX, None, ALU.mult))
            ch(v.reciprocal(scp[0:1, 6:7], scp[0:1, 4:5]))
            P.inc(v.engine_nop(), "dv")
            P.sems["chv"][1] = ch.n
        blk.vector(_)

        def _(pe):
            P.wait(pe, "dv")
            P.inc(pe.matmul(pm[0:128, 0:1], one128[0:1, :], scp[0:1, 6:7],
                            start=True, stop=True), "pe")
        blk.tensor(_)

        def _(v):
            P.wait(v, "pe")
            P.inc(v.tensor_copy(iv1[:, 0:1], pm[0:128, 0:1]), "dv")
        blk.vector(_)

        def _(pe):
            P.wait(pe, "dv")
            P.inc(pe.matmul(pm[0:128, 1:2], one128[0:1, :], scp[0:1, 4:5],
                            start=True, stop=True), "pe")
        blk.tensor(_)

        def _(v):
            ch = Chain(v, P.sems["chv"][0])
            ch.n = P.n("chv")
            P.wait(v, "pe")
            ch(v.tensor_copy(iv1[:, 1:2], pm[0:128, 1:2]))
            ch(v.tensor_scalar(sxg[:, :], wgss[:, :], iv1[:, 1:2], None, ALU.mult))
            ch(v.tensor_scalar(sxu[:, :], wuss[:, :], iv1[:, 1:2], None, ALU.mult))
            P.inc(v.engine_nop(), "dv")
            P.sems["chv"][1] = ch.n
        blk.vector(_)

        for m in range(NM):
            b, j = divmod(m, TOK // 128)
            sl = m % 2
            lsem = "lxA" if sl == 0 else "lxB"
            qsem = "sqA" if sl == 0 else "sqB"

            def _(s, b=b, j=j, sl=sl, m=m, lsem=lsem):
                if m >= 2:
                    P.wait(s, "dv")
                P.inc(s.dma_start(out=x1b[:, sl * D:(sl + 1) * D],
                                  in_=x1sp[b, j * 128:(j + 1) * 128, :]), lsem, 16)
            blk.sync(_)

            def _(a, sl=sl, m=m, lsem=lsem):
                P.wait(a, lsem)
                if m >= 1:
                    P.wait(a, "dv", P.n("dv") - 1)
                P.inc(a.activation(sscr[:, :], x1b[:, sl * D:(sl + 1) * D],
                                   AF.Square, accum_out=stat[:, 0:1]), "ac")
                P.wait(a, "ac")
                P.inc(a.activation(sscr[:, :], x1b[:, sl * D:(sl + 1) * D],
                                   AF.Identity, scale=1.0,
                                   accum_out=stat[:, 1:2]), "ac")
            blk.scalar(_)

            def _(v, m=m):
                ch = Chain(v, P.sems["chv"][0])
                ch.n = P.n("chv")
                P.wait(v, "ac")
                ch(v.tensor_scalar(stat[:, 2:3], stat[:, 1:2], 1.0 / D, None, ALU.mult))
                ch(v.tensor_scalar(stat[:, 4:5], stat[:, 0:1], 1.0 / D, None, ALU.mult))
                ch(v.tensor_tensor(stat[:, 5:6], stat[:, 2:3], stat[:, 2:3], ALU.mult))
                ch(v.tensor_tensor(stat[:, 3:4], stat[:, 4:5], stat[:, 5:6],
                                   ALU.subtract))
                P.inc(v.engine_nop(), "dv")
                P.sems["chv"][1] = ch.n
            blk.vector(_)

            def _(a):
                P.wait(a, "dv")
                P.inc(a.activation(stdv[:, 0:1], stat[:, 3:4], AF.Sqrt,
                                   bias=epsv[:, 0:1]), "ac")
            blk.scalar(_)

            def _(v, sl=sl, m=m, qsem=qsem):
                ch = Chain(v, P.sems["chv"][0])
                ch.n = P.n("chv")
                P.wait(v, "ac")
                ch(v.reciprocal(rstdv[:, 0:1], stdv[:, 0:1]))
                ch(v.tensor_scalar(rstd2[:, 0:1], rstdv[:, 0:1], iv1[:, 0:1],
                                   None, ALU.mult))
                ch(v.tensor_scalar(stat[:, 6:7], stat[:, 2:3], rstd2[:, 0:1],
                                   -1.0, ALU.mult, ALU.mult))
                ch(v.tensor_scalar(uscr[:, :], x1b[:, sl * D:(sl + 1) * D],
                                   rstd2[:, 0:1], stat[:, 6:7], ALU.mult, ALU.add))
                ch(v.tensor_scalar(uscr[:, :], uscr[:, :], RND_C, RND_C,
                                   ALU.add, ALU.subtract))
                if m >= 2:
                    P.wait(v, qsem, P.n(qsem))
                ch(v.tensor_scalar(xqt[:, sl * D:(sl + 1) * D], uscr[:, :],
                                   QMAX, -QMAX - 1.0, ALU.min, ALU.max))
                P.inc(v.engine_nop(), "dv")
                P.sems["chv"][1] = ch.n
            blk.vector(_)

            def _(g, sl=sl, m=m, qsem=qsem):
                P.wait(g, "dv")
                P.inc(g.dma_start(out=xqscr[m * 128:(m + 1) * 128, :],
                                  in_=xqt[:, sl * D:(sl + 1) * D]), qsem, 16)
            blk.gpsimd(_)

        lg_total = P.n("lg")
        pe_base = P.n("pe")
        ac_base = P.n("ac")
        dv_base = P.n("dv")
        PEG = 2 * 4 * NIC

        for g_i in range(NG):
            gsl = g_i % 2
            tsem = "tqA" if gsl == 0 else "tqB"

            def _(s, g_i=g_i, gsl=gsl, tsem=tsem):
                if g_i == 0:
                    P.wait(s, "sqA", P.n("sqA"))
                    P.wait(s, "sqB", P.n("sqB"))
                if g_i >= 2:
                    P.wait(s, "pe", pe_base + PEG * (g_i - 1))
                for k in range(8):
                    P.inc(s.dma_start(
                        out=xqf[:, (gsl * 8 + k) * 2048:(gsl * 8 + k + 1) * 2048],
                        in_=xqscr[g_i * 2048:(g_i + 1) * 2048, k * 128:(k + 1) * 128],
                        transpose=True), tsem, 16)
            blk.sync(_)

            for fg in range(4):
                for ic in range(NIC):
                    t = (g_i * 4 + fg) * NIC + ic
                    fsl = t % 2
                    pa = pa0 if fsl == 0 else pa1
                    pb = pb0 if fsl == 0 else pb1
                    ssem = "stA" if fsl == 0 else "stB"

                    def _(pe, g_i=g_i, gsl=gsl, fg=fg, ic=ic, t=t, pa=pa, pb=pb,
                          tsem=tsem):
                        if t == 0:
                            P.wait(pe, "lg", lg_total)
                        P.wait(pe, tsem, 128 * (g_i // 2 + 1))
                        if t >= 2:
                            P.wait(pe, "ac", ac_base + t - 1)
                            P.wait(pe, "dv", dv_base + (t - 1))
                        for kc in range(8):
                            mm = pe.matmul(pa[:, :],
                                           wgf[0:128, kc * IFI + ic * 128:
                                               kc * IFI + (ic + 1) * 128],
                                           xqf[:, (gsl * 8 + kc) * 2048 + fg * 512:
                                               (gsl * 8 + kc) * 2048 + (fg + 1) * 512],
                                           start=(kc == 0), stop=(kc == 7))
                        P.inc(mm, "pe")
                        for kc in range(8):
                            mm = pe.matmul(pb[:, :],
                                           wuf[0:128, kc * IFI + ic * 128:
                                               kc * IFI + (ic + 1) * 128],
                                           xqf[:, (gsl * 8 + kc) * 2048 + fg * 512:
                                               (gsl * 8 + kc) * 2048 + (fg + 1) * 512],
                                           start=(kc == 0), stop=(kc == 7))
                        P.inc(mm, "pe")
                    blk.tensor(_)

                    def _(a, ic=ic, t=t, fsl=fsl, pa=pa):
                        P.wait(a, "pe", pe_base + 2 * t + 1)
                        if t >= 2:
                            P.wait(a, "dv", dv_base + t - 1)
                        P.inc(a.activation(gsil[:, fsl * 512:(fsl + 1) * 512],
                                           pa[:, :], AF.Silu,
                                           scale=sxg[:, ic: ic + 1]), "ac")
                    blk.scalar(_)

                    def _(v, ic=ic, t=t, fsl=fsl, pb=pb, ssem=ssem):
                        ch = Chain(v, P.sems["chv"][0])
                        ch.n = P.n("chv")
                        P.wait(v, "pe", pe_base + 2 * t + 2)
                        ch(v.tensor_scalar(uf32[:, fsl * 512:(fsl + 1) * 512],
                                           pb[:, :], sxu[:, ic: ic + 1], None,
                                           ALU.mult))
                        P.wait(v, "ac", ac_base + t + 1)
                        if t >= 2:
                            P.wait(v, ssem, 16 * (t // 2))
                        ch(v.tensor_tensor(sguf[:, fsl * 512:(fsl + 1) * 512],
                                           gsil[:, fsl * 512:(fsl + 1) * 512],
                                           uf32[:, fsl * 512:(fsl + 1) * 512],
                                           ALU.mult))
                        ch(v.tensor_reduce(stat[:, 7:8],
                                           sguf[:, fsl * 512:(fsl + 1) * 512],
                                           AX.X, ALU.max, apply_absolute_value=True))
                        ch(v.tensor_tensor(gacc[:, 0:1], gacc[:, 0:1], stat[:, 7:8],
                                           ALU.max))
                        P.inc(v.engine_nop(), "dv")
                        P.sems["chv"][1] = ch.n
                    blk.vector(_)

                    def _(g, g_i=g_i, fg=fg, ic=ic, fsl=fsl, ssem=ssem):
                        P.wait(g, "dv")
                        P.inc(g.dma_start(
                            out=sgusp[ic, :, g_i * 2048 + fg * 512:
                                      g_i * 2048 + (fg + 1) * 512],
                            in_=sguf[:, fsl * 512:(fsl + 1) * 512]), ssem, 16)
                    blk.gpsimd(_)

        def _(pe):
            P.wait(pe, "dv")
            P.inc(pe.transpose(pm[0:1, 0:128], gacc[:, 0:1], idf[:, :]), "pe")
        blk.tensor(_)

        def _(v):
            P.wait(v, "pe")
            ch = Chain(v, P.sems["chv"][0])
            ch.n = P.n("chv")
            ch(v.tensor_reduce(gfin[0:1, 0:1], pm[0:1, 0:128], AX.X, ALU.max))
            P.inc(v.engine_nop(), "dv")
            P.sems["chv"][1] = ch.n
        blk.vector(_)

        def _(g):
            P.wait(g, "dv")
            P.inc(g.dma_start(out=gmax2p[:, :], in_=gfin[0:1, 0:1]), "stG", 16)
            P.wait(g, "stA")
            P.wait(g, "stB")
            P.wait(g, "stG")
        blk.gpsimd(_)

    return nc


def host_rF(inputs, rA_cat, rE_outs, cid):
    return {
        "x1sp": np.asarray(rE_outs[cid]["x1sp"]),
        "gmax18": np.concatenate([np.asarray(rE_outs[c]["gmax1p"]) for c in range(8)],
                                 axis=1),
        "clipx": np.asarray(inputs["L0_ffn_clip_x"]).reshape(1, 1),
        "wgq": rA_cat["wgq"],
        "wuq": rA_cat["wuq"],
        "wgs": rA_cat["wgs"].reshape(IFI, 1),
        "wus": rA_cat["wus"].reshape(IFI, 1),
        "idenF": np.eye(128, dtype=np.float32),
    }




def build_rG():
    nc = _bass()
    x1sp = nc.dram_tensor("x1sp", [B, TOK, D], F32, kind="ExternalInput")
    sgusp = nc.dram_tensor("sgusp", [22, 128, B * TOK], F32, kind="ExternalInput")
    gmax28 = nc.dram_tensor("gmax28", [1, 8], F32, kind="ExternalInput")
    clipd = nc.dram_tensor("clipd", [1, 1], F32, kind="ExternalInput")
    wdq = nc.dram_tensor("wdq", [D, IFI], BF16, kind="ExternalInput")
    wdsv = nc.dram_tensor("wdsv", [1, D], F32, kind="ExternalInput")

    xout = nc.dram_tensor("xout", [B, TOK, D], F32, kind="ExternalOutput")

    NGR = B * TOK // 512
    NIC = IFI // 128
    with ExitStack() as ctx:
        def sb(name, shape, dt=F32):
            return ctx.enter_context(nc.sbuf_tensor(name, shape, dt))

        def psum(name, shape, dt=F32):
            return ctx.enter_context(nc.psum_tensor(name, shape, dt))

        wdf = sb("wdf", [128, NIC * D], BF16)
        sgub = sb("sgub", [128, NIC * 512], F32)
        hfm = sb("hfm", [128, NIC * 512], BF16)
        x1b = sb("x1bG", [128, 2 * 4 * D], F32)
        outs = sb("outsG", [128, 2 * D], F32)
        u2 = sb("u2G", [128, 512], F32)
        sc2m = sb("sc2m", [128, D], F32)
        wdss = sb("wdss", [1, D], F32)
        wss2 = sb("wss2", [1, D], F32)
        g28 = sb("g28", [1, 8], F32)
        cld = sb("cldG", [1, 2], F32)
        scp = sb("scpG", [1, 8], F32)
        iv2 = sb("iv2G", [128, 2], F32)
        one128 = sb("one128G", [1, 128], F32)
        pp0 = psum("pp0G", [128, 512])
        pp1 = psum("pp1G", [128, 512])
        pm = psum("pmG", [128, 512])
        blk = ctx.enter_context(nc.Block())
        P = Prog(nc, blk, ctx)
        P.sem("lw", "ld", "lsg", "lxA", "lxB", "pe", "dv", "dq", "ac",
              "soA", "soB", "chv")

        def _(s):
            P.inc(s.dma_start(out=g28[:, :], in_=gmax28[:, :]), "ld", 16)
            P.inc(s.dma_start(out=cld[:, 0:1], in_=clipd[:, :]), "ld", 16)
            P.inc(s.dma_start(out=wdss[:, :], in_=wdsv[:, :]), "ld", 16)
        blk.sync(_)

        def _(a):
            for ic in range(NIC):
                P.inc(a.dma_start(out=wdf[:, ic * D:(ic + 1) * D],
                                  in_=wdq[:, ic * 128:(ic + 1) * 128],
                                  transpose=True), "lw", 16)
        blk.scalar(_)

        ld_total = P.n("ld")

        def _(a):
            P.wait(a, "ld", ld_total)
            P.inc(a.activation(cld[0:1, 1:2], cld[0:1, 0:1], AF.Sigmoid), "ac")
        blk.scalar(_)

        def _(v):
            ch = Chain(v, P.sems["chv"][0])
            ch.n = P.n("chv")
            P.wait(v, "ac")
            ch(v.memset(one128[:, :], 1.0))
            ch(v.tensor_reduce(scp[0:1, 0:1], g28[0:1, :], AX.X, ALU.max))
            ch(v.tensor_tensor(scp[0:1, 2:3], scp[0:1, 0:1], cld[0:1, 1:2], ALU.mult))
            ch(v.tensor_scalar(scp[0:1, 4:5], scp[0:1, 2:3], 1.0 / QMAX, None,
                               ALU.mult))
            ch(v.reciprocal(scp[0:1, 6:7], scp[0:1, 4:5]))
            ch(v.tensor_scalar(wss2[0:1, :], wdss[0:1, :], scp[0:1, 4:5], None,
                               ALU.mult))
            P.inc(v.engine_nop(), "dv")
            P.sems["chv"][1] = ch.n
        blk.vector(_)

        def _(pe):
            P.wait(pe, "dv")
            mm = pe.matmul(pm[0:128, 0:1], one128[0:1, :], scp[0:1, 6:7],
                           start=True, stop=True)
            P.inc(mm, "pe")
        blk.tensor(_)

        def _(v):
            P.wait(v, "pe")
            P.inc(v.tensor_copy(iv2[:, 0:1], pm[0:128, 0:1]), "dv")
        blk.vector(_)

        for half in range(2):
            def _(pe, half=half):
                P.wait(pe, "dv")
                mm = pe.matmul(pm[0:128, 0:512], one128[0:1, :],
                               wss2[0:1, half * 512:(half + 1) * 512],
                               start=True, stop=True)
                P.inc(mm, "pe")
            blk.tensor(_)

            def _(v, half=half):
                P.wait(v, "pe")
                P.inc(v.tensor_copy(sc2m[:, half * 512:(half + 1) * 512],
                                    pm[0:128, 0:512]), "dv")
            blk.vector(_)

        lw_total = P.n("lw")
        pre_pe = P.n("pe")
        pre_dv = P.n("dv")

        for g in range(NGR):
            b, hlf = divmod(g, 2)
            gsl = g % 2
            lxs = "lxA" if gsl == 0 else "lxB"

            def _(s, g=g):
                if g >= 1:
                    P.wait(s, "dq", g * NIC)
                for ic in range(NIC):
                    P.inc(s.dma_start(out=sgub[:, ic * 512:(ic + 1) * 512],
                                      in_=sgusp[ic, :, g * 512:(g + 1) * 512]),
                          "lsg", 16)
            blk.sync(_)

            def _(s, g=g, b=b, hlf=hlf, gsl=gsl, lxs=lxs):
                if g >= 2:
                    P.wait(s, "dv", pre_dv + 8 * (g - 1))
                for tt in range(4):
                    P.inc(s.dma_start(
                        out=x1b[:, (gsl * 4 + tt) * D:(gsl * 4 + tt + 1) * D],
                        in_=x1sp[b, hlf * 512 + tt * 128: hlf * 512 + (tt + 1) * 128,
                                 :]), lxs, 16)
            blk.sync(_)

            for ic in range(NIC):
                def _(v, g=g, ic=ic):
                    ch = Chain(v, P.sems["chv"][0])
                    ch.n = P.n("chv")
                    if ic == 0:
                        P.wait(v, "lsg", 16 * NIC * (g + 1))
                        if g >= 1:
                            P.wait(v, "pe", pre_pe + 8 * g)
                    ch(v.tensor_scalar(u2[:, :], sgub[:, ic * 512:(ic + 1) * 512],
                                       iv2[:, 0:1], None, ALU.mult))
                    ch(v.tensor_scalar(u2[:, :], u2[:, :], RND_C, RND_C,
                                       ALU.add, ALU.subtract))
                    ch(v.tensor_scalar(hfm[:, ic * 512:(ic + 1) * 512], u2[:, :],
                                       QMAX, -QMAX - 1.0, ALU.min, ALU.max))
                    P.inc(v.engine_nop(), "dq")
                    P.sems["chv"][1] = ch.n
                blk.vector(_)

            for tt in range(4):
                tsl = tt % 2
                osem = "soA" if tsl == 0 else "soB"
                for nfg in range(2):
                    j = tt * 2 + nfg
                    pp = pp0 if j % 2 == 0 else pp1

                    def _(pe, g=g, j=j, tt=tt, nfg=nfg, pp=pp):
                        if j == 0:
                            if g == 0:
                                P.wait(pe, "lw", lw_total)
                            P.wait(pe, "dq", (g + 1) * NIC)
                        tglob = g * 8 + j
                        if tglob >= 2:
                            P.wait(pe, "dv", pre_dv + tglob - 1)
                        for ic in range(NIC):
                            mm = pe.matmul(pp[:, :],
                                           hfm[0:128, ic * 512 + tt * 128:
                                               ic * 512 + (tt + 1) * 128],
                                           wdf[0:128, ic * D + nfg * 512:
                                               ic * D + (nfg + 1) * 512],
                                           start=(ic == 0), stop=(ic == NIC - 1))
                        P.inc(mm, "pe")
                    blk.tensor(_)

                    def _(v, g=g, gsl=gsl, tt=tt, tsl=tsl, nfg=nfg, j=j, pp=pp,
                          lxs=lxs, osem=osem):
                        ch = Chain(v, P.sems["chv"][0])
                        ch.n = P.n("chv")
                        P.wait(v, "pe")
                        if nfg == 0:
                            P.wait(v, lxs, 64 * (g // 2 + 1))
                            if g * 4 + tt >= 2:
                                P.wait(v, osem, 16 * ((g * 4 + tt) // 2))
                        ch(v.tensor_tensor(outs[:, tsl * D + nfg * 512:
                                                tsl * D + (nfg + 1) * 512],
                                           pp[:, :],
                                           sc2m[:, nfg * 512:(nfg + 1) * 512],
                                           ALU.mult))
                        ch(v.tensor_tensor(outs[:, tsl * D + nfg * 512:
                                                tsl * D + (nfg + 1) * 512],
                                           outs[:, tsl * D + nfg * 512:
                                                tsl * D + (nfg + 1) * 512],
                                           x1b[:, (gsl * 4 + tt) * D + nfg * 512:
                                               (gsl * 4 + tt) * D + (nfg + 1) * 512],
                                           ALU.add))
                        P.inc(v.engine_nop(), "dv")
                        P.sems["chv"][1] = ch.n
                    blk.vector(_)

                def _(gp, g=g, b=b, hlf=hlf, tt=tt, tsl=tsl, osem=osem):
                    P.wait(gp, "dv")
                    P.inc(gp.dma_start(
                        out=xout[b, hlf * 512 + tt * 128: hlf * 512 + (tt + 1) * 128,
                                 :],
                        in_=outs[:, tsl * D:(tsl + 1) * D]), osem, 16)
                blk.gpsimd(_)

        def _(gp):
            P.wait(gp, "soA")
            P.wait(gp, "soB")
        blk.gpsimd(_)

    return nc


def host_rG(inputs, rA_cat, rE_outs, rF_outs, cid):
    return {
        "x1sp": np.asarray(rE_outs[cid]["x1sp"]),
        "sgusp": np.asarray(rF_outs[cid]["sgusp"]),
        "gmax28": np.concatenate([np.asarray(rF_outs[c]["gmax2p"]) for c in range(8)],
                                 axis=1),
        "clipd": np.asarray(inputs["L0_ffn_clip_down"]).reshape(1, 1),
        "wdq": rA_cat["wdq"],
        "wdsv": rA_cat["wds"].reshape(1, D),
    }



_NC_CACHE = {}


def _get_nc(name):
    if name not in _NC_CACHE:
        _NC_CACHE[name] = globals()[f"build_r{name}"]()
    return _NC_CACHE[name]


def kernel(**inputs):
    cores = list(range(NCORES))

    res = run_bass_kernel_spmd(_get_nc("A"), host_rA(inputs), core_ids=cores)
    rA_outs = [res.results[c] for c in range(NCORES)]
    rA_cat = {k: np.concatenate([np.asarray(rA_outs[c][k]) for c in range(NCORES)],
                                axis=0)
              for k in ("wgq", "wuq", "wdq", "wgs", "wus", "wds")}

    res = run_bass_kernel_spmd(_get_nc("B"), host_rB(inputs, rA_outs),
                               core_ids=cores)
    rB_outs = [{"outc": res.results[h]["outc"]} for h in range(NCORES)]

    res = run_bass_kernel_spmd(_get_nc("C"),
                               [host_rC(inputs, rA_outs, rB_outs, c)
                                for c in range(NCORES)], core_ids=cores)
    rC_outs = [res.results[c] for c in range(NCORES)]
    qe = np.concatenate([rC_outs[c]["qesl"] for c in range(NCORES)], axis=0)
    ke = np.concatenate([rC_outs[c]["kesl"] for c in range(NCORES)], axis=0)

    res = run_bass_kernel_spmd(_get_nc("D"),
                               [host_rD(inputs, qe, ke, c)
                                for c in range(NCORES)], core_ids=cores)
    rD_outs = [{"arawp": res.results[c]["arawp"]} for c in range(NCORES)]

    res = run_bass_kernel_spmd(_get_nc("E"),
                               [host_rE(inputs, rA_outs, rC_outs, rD_outs, c)
                                for c in range(NCORES)], core_ids=cores)
    rE_outs = [res.results[c] for c in range(NCORES)]

    res = run_bass_kernel_spmd(_get_nc("F"),
                               [host_rF(inputs, rA_cat, rE_outs, c)
                                for c in range(NCORES)], core_ids=cores)
    rF_outs = [res.results[c] for c in range(NCORES)]

    res = run_bass_kernel_spmd(_get_nc("G"),
                               [host_rG(inputs, rA_cat, rE_outs, rF_outs, c)
                                for c in range(NCORES)], core_ids=cores)
    out = np.concatenate([np.asarray(res.results[c]["xout"], dtype=np.float32)
                          for c in range(NCORES)], axis=1)
    return np.ascontiguousarray(out)
